# version 03
# baseline (speedup 1.0000x reference)
import contextlib
import numpy as np
import concourse.bass as bass
import concourse.mybir as mybir
from concourse.bass_utils import run_bass_kernel_spmd

F32 = mybir.dt.float32
BF16 = mybir.dt.bfloat16
AF = mybir.ActivationFunctionType
ALU = mybir.AluOpType
AX = mybir.AxisListType
ET = mybir.EngineType

P = 128
SQ = 8192
H = 2048
TOK = 1024
NT = TOK // P
HC = H // P
NQB = SQ // 512
NKC = SQ // P
GL = 2
RG = [[0, 1, 2, 3, 4, 5, 6, 7]]
INV_SQRT_HN = 0.08838834764831845
EPS = 1e-5

ENGINES = ("sync", "tensor", "scalar", "vector", "gpsimd")


class B:

    def __init__(self, nc, stk):
        self.nc = nc
        self.prog = {e: [] for e in ENGINES}
        self.sem = {}
        self.ctr = {}
        self.waited = {e: {} for e in ENGINES}
        for name in ("sx", "sg", "c1", "c2", "t", "s", "v"):
            self.sem[name] = stk.enter_context(nc.semaphore(name))
            self.ctr[name] = 0

    def w(self, e, s, val):
        if not val or val <= 0:
            return
        if self.waited[e].get(s, 0) >= val:
            return
        self.waited[e][s] = val
        h = self.sem[s]
        self.prog[e].append(lambda eng, h=h, val=val: eng.wait_ge(h, val))

    def wt(self, e, tag):
        if tag:
            self.w(e, tag[0], tag[1])

    def o(self, e, fn):
        self.prog[e].append(fn)

    def oi(self, e, fn, s, k=1):
        self.ctr[s] += k
        n = self.ctr[s]
        h = self.sem[s]
        self.prog[e].append(lambda eng, fn=fn, h=h, k=k: fn(eng).then_inc(h, k))
        return n

    def oc(self, e, fn, s):
        self.ctr[s] = max(self.ctr[s], 1)
        h = self.sem[s]
        self.prog[e].append(lambda eng, fn=fn, h=h: fn(eng).then_inc(h))
        return 1

    def dma(self, e, out, in_, s=None):
        if s is None:
            s = "sg" if e == "gpsimd" else "sx"
        return self.oi(e, lambda eng, out=out, in_=in_: eng.dma_start(out=out, in_=in_), s, 16)


def build(stage="full"):
    nc = bass.Bass("TRN2", target_bir_lowering=False, num_devices=8)

    xs = nc.dram_tensor("xs", [TOK, H], F32, kind="ExternalInput").ap()
    wqkvT = nc.dram_tensor("wqkvT", [H, GL * 3 * P], F32, kind="ExternalInput").ap()
    wprojT = nc.dram_tensor("wprojT", [H, H], F32, kind="ExternalInput").ap()
    bproj = nc.dram_tensor("bproj", [1, H], F32, kind="ExternalInput").ap()
    w1 = nc.dram_tensor("w1", [H, H], F32, kind="ExternalInput").ap()
    b1c = nc.dram_tensor("b1c", [P, HC], F32, kind="ExternalInput").ap()
    w2 = nc.dram_tensor("w2", [H, 8], F32, kind="ExternalInput").ap()
    b2 = nc.dram_tensor("b2", [1, 8], F32, kind="ExternalInput").ap()
    vflatT = nc.dram_tensor("vflatT", [H, 512], F32, kind="ExternalInput").ap()
    uflat = nc.dram_tensor("uflat", [512, H], F32, kind="ExternalInput").ap()
    sel = nc.dram_tensor("sel", [8, 512], F32, kind="ExternalInput").ap()
    ident = nc.dram_tensor("ident", [P, P], F32, kind="ExternalInput").ap()
    out = nc.dram_tensor("out", [TOK, H], F32, kind="ExternalOutput").ap()

    cc1_in = nc.dram_tensor("cc1_in", [H, TOK], BF16, kind="Internal").ap()
    cc1_out = nc.dram_tensor("cc1_out", [8 * H, TOK], BF16, kind="Internal").ap()
    cc2_in = nc.dram_tensor("cc2_in", [GL * P, SQ], BF16, kind="Internal").ap()
    cc2_out = nc.dram_tensor("cc2_out", [8 * GL * P, SQ], BF16, kind="Internal").ap()

    dbg = {}
    if stage == "qkv":
        for nm in ("d_qT", "d_kT", "d_v"):
            dbg[nm] = nc.dram_tensor(nm, [P, SQ], F32, kind="ExternalOutput").ap()
        dbg["d_wq"] = nc.dram_tensor("d_wq", [P, HC * GL * 3 * P], F32, kind="ExternalOutput").ap()
    if stage == "ctx":
        dbg["d_ctx"] = nc.dram_tensor("d_ctx", [GL * P, SQ], F32, kind="ExternalOutput").ap()
    if stage == "a":
        dbg["d_ln1T"] = nc.dram_tensor("d_ln1T", [H, TOK], F32, kind="ExternalOutput").ap()

    stk = contextlib.ExitStack()
    with stk:
        b = B(nc, stk)
        sb = lambda name, shape, dt: stk.enter_context(nc.sbuf_tensor(name, shape, dt))

        ident_bf = sb("ident_bf", [P, P], BF16)
        ones_col_f = sb("ones_col_f", [P, 1], F32)
        ones_col_bf = sb("ones_col_bf", [P, 1], BF16)
        ones_row_f = sb("ones_row_f", [1, P], F32)
        bprojrow = sb("bprojrow", [1, H], F32)
        b_sb = sb("b_sb", [P, H], F32)
        b1c_sb = sb("b1c_sb", [P, HC], F32)
        w2_sb = sb("w2_sb", [P, HC * 8], BF16)
        b2row = sb("b2row", [1, 8], F32)
        b2b_sb = sb("b2b_sb", [P, 8], F32)
        sel_sb = sb("sel_sb", [8, 512], BF16)
        alphaT_sb = sb("alphaT_sb", [8, TOK], BF16)
        eps_sb = sb("eps_sb", [P, 1], F32)

        n_id = b.dma("gpsimd", ident_bf[:], ident[:])
        b.dma("sync", b1c_sb[:], b1c[:])
        b.dma("sync", bprojrow[:], bproj[:])
        b.dma("sync", b2row[:], b2[:])
        b.dma("gpsimd", sel_sb[:], sel[:])
        n_w2l = b.dma("gpsimd", w2_sb[:].rearrange("p (c k) -> p c k", k=8),
                      w2[:].rearrange("(c p) k -> p c k", p=P))
        b.oi("vector", lambda e: e.memset(ones_col_f[:], 1.0), "v")
        b.oi("vector", lambda e: e.memset(ones_col_bf[:], 1.0), "v")
        n_ones = b.oi("vector", lambda e: e.memset(ones_row_f[:], 1.0), "v")
        b.oi("vector", lambda e: e.memset(eps_sb[:], EPS), "v")

        pid_cell = {}
        def _getpid(eng):
            pid_cell["v"] = nc.partition_id(engines=[ET.SP])
        b.o("sync", _getpid)

        with contextlib.ExitStack() as stkW:
            sbW = lambda name, shape, dt: stkW.enter_context(nc.sbuf_tensor(name, shape, dt))
            wqkv_sb = sbW("wqkv_sb", [P, HC * GL * 3 * P], BF16)
            n_wq = b.dma("gpsimd", wqkv_sb[:].rearrange("p (c f) -> p c f", f=GL * 3 * P),
                         wqkvT[:].rearrange("(c p) f -> p c f", p=P))

            with contextlib.ExitStack() as stkA:
                sbA = lambda name, shape, dt: stkA.enter_context(nc.sbuf_tensor(name, shape, dt))
                psA = lambda name, shape, dt: stkA.enter_context(nc.psum_tensor(name, shape, dt))
                xtile = [sbA(f"xtile{i}", [P, H], F32) for i in range(2)]
                ln1_bf = [sbA(f"ln1bf{i}", [P, H], BF16) for i in range(2)]
                sq_scr = sbA("sq_scr", [P, H], F32)
                ln1T_sb = sbA("ln1T", [P, HC * TOK], BF16)
                ssum = sbA("ssum", [P, 1], F32)
                negmean = [sbA(f"negmean{i}", [P, 1], F32) for i in range(2)]
                varsum = [sbA(f"varsum{i}", [P, 1], F32) for i in range(2)]
                stdv = [sbA(f"stdv{i}", [P, 1], F32) for i in range(2)]
                rstd = [sbA(f"rstd{i}", [P, 1], F32) for i in range(2)]
                nmr = [sbA(f"nmr{i}", [P, 1], F32) for i in range(2)]
                psum_trA = [psA(f"pstrA{i}", [P, P], BF16) for i in range(2)]

                n_xload = [0] * NT
                n_ln1 = [0] * NT
                n_trlast = [0] * NT
                trA_free = {0: 0, 1: 0}
                last_evac = 0
                for tt in range(NT):
                    if tt >= 2:
                        b.w("sync", "s", n_ln1[tt - 2])
                    n_xload[tt] = b.dma("sync", xtile[tt % 2][:], xs[tt * P:(tt + 1) * P, :])

                    if tt >= 2:
                        b.w("vector", "s", n_ln1[tt - 2])
                    b.w("vector", "sx", n_xload[tt])
                    n_red = b.oi("vector", lambda e, x=xtile[tt % 2]:
                                 e.reduce_sum(out=ssum[:], in_=x[:], axis=AX.X), "v")
                    b.w("vector", "v", n_red)
                    n_nm = b.oi("vector", lambda e, nm=negmean[tt % 2]:
                                e.tensor_single_scalar(out=nm[:], in_=ssum[:], scalar=-1.0 / H,
                                                       op=ALU.mult), "v")

                    b.w("scalar", "v", n_nm)
                    if tt >= 2:
                        b.w("scalar", "t", n_trlast[tt - 2])
                    n_sqa = b.oi("scalar", lambda e, x=xtile[tt % 2], nm=negmean[tt % 2], vs=varsum[tt % 2]:
                                 e.activation(out=sq_scr[:], in_=x[:], func=AF.Square, bias=nm[:], scale=1.0,
                                              accum_out=vs[:]), "s")
                    b.w("scalar", "s", n_sqa)
                    n_std = b.oi("scalar", lambda e, vs=varsum[tt % 2], sd=stdv[tt % 2]:
                                 e.activation(out=sd[:], in_=vs[:], func=AF.Sqrt, bias=eps_sb[:],
                                              scale=1.0 / H), "s")

                    b.w("vector", "s", n_std)
                    n_rec = b.oi("vector", lambda e, sd=stdv[tt % 2], rs=rstd[tt % 2]:
                                 e.reciprocal(out=rs[:], in_=sd[:]), "v")
                    b.w("vector", "v", n_rec)
                    n_nmr = b.oi("vector", lambda e, nm=negmean[tt % 2], rs=rstd[tt % 2], m=nmr[tt % 2]:
                                 e.tensor_mul(out=m[:], in0=nm[:], in1=rs[:]), "v")

                    b.w("scalar", "v", n_nmr)
                    n_ln1[tt] = b.oi("scalar", lambda e, x=xtile[tt % 2], rs=rstd[tt % 2],
                                     m=nmr[tt % 2], o=ln1_bf[tt % 2]:
                                     e.activation(out=o[:], in_=x[:], func=AF.Identity,
                                                  bias=m[:], scale=rs[:]), "s")

                    b.w("tensor", "s", n_ln1[tt])
                    b.w("tensor", "sg", n_id)
                    for hc in range(HC):
                        pb = hc % 2
                        b.w("tensor", "v", trA_free[pb])
                        n_trlast[tt] = b.oi("tensor", lambda e, i=ln1_bf[tt % 2], h=hc, o=psum_trA[pb]:
                                            e.transpose(out=o[:], in_=i[:, h * P:(h + 1) * P],
                                                        identity=ident_bf[:]), "t")
                        b.w("vector", "t", n_trlast[tt])
                        last_evac = b.oi("vector", lambda e, h=hc, t_=tt, i=psum_trA[pb]:
                                         e.tensor_copy(out=ln1T_sb[:, h * TOK + t_ * P:
                                                                   h * TOK + (t_ + 1) * P],
                                                       in_=i[:]), "v")
                        trA_free[pb] = last_evac

                b.w("gpsimd", "v", last_evac)
                n_cc1in = b.dma("gpsimd",
                                cc1_in[:].rearrange("(c p) t -> p c t", p=P),
                                ln1T_sb[:].rearrange("p (c t) -> p c t", t=TOK))
                b.w("gpsimd", "sg", n_cc1in)
                b.oc("gpsimd", lambda e: e.collective_compute(
                    "AllGather", ALU.bypass, replica_groups=RG,
                    ins=[cc1_in[:]], outs=[cc1_out[:]]), "c1")
                if stage == "a":
                    b.dma("gpsimd", dbg["d_ln1T"][:].rearrange("(c p) t -> p c t", p=P),
                          ln1T_sb[:].rearrange("p (c t) -> p c t", t=TOK))
                    b.w("gpsimd", "sg", b.ctr["sg"])


            with contextlib.ExitStack() as stkB:
                sbB = lambda name, shape, dt: stkB.enter_context(nc.sbuf_tensor(name, shape, dt))
                psB = lambda name, shape, dt: stkB.enter_context(nc.psum_tensor(name, shape, dt))
                rhs_sb = [sbB(f"rhs{i}", [P, HC * 512], BF16) for i in range(2)]
                qT_sb = sbB("qT", [P, SQ], BF16)
                kT_sb = sbB("kT", [P, SQ], BF16)
                v_sb = sbB("v_sb", [P, SQ], BF16)
                ctxT_sb = sbB("ctxT", [P, GL * SQ], BF16)
                vstage = [sbB(f"vstage{i}", [P, 512], BF16) for i in range(2)]
                pT_sb = [sbB(f"pT{i}", [P, 1024], BF16) for i in range(2)]
                tmp_bf = sbB("tmp_bf", [P, 512], BF16)
                Tacc = [sbB(f"Tacc{i}", [P, 512], F32) for i in range(2)]
                invden = sbB("invden", [1, 512], F32)
                bcast_sb = [sbB(f"bcast{i}", [P, 512], F32) for i in range(2)]
                sq_bf = sbB("sq_bf", [P, H], BF16)
                negb_sb = sbB("negb", [P, 1], F32)
                m11 = sbB("m11", [1, 1], F32)
                mx2 = [sbB(f"mx2_{i}", [1, 1], F32) for i in range(2)]
                bq11 = sbB("bq11", [1, 1], F32)
                psum_s = [psB(f"psS{i}", [P, 1024], F32) for i in range(2)]
                psum_ctx = [psB(f"psctx{i}", [P, 512], F32) for i in range(2)]
                psum_bc = psB("psbc", [P, 512], F32)
                psum_trB = psB("pstrB", [P, 512], BF16)

                n_rhs = {}
                n_qkvmm = {}
                n_evac = {}
                n_vtr = [0] * (GL * NQB + 2)
                n_vevac = [0] * (GL * NQB + 2)
                sfree = {0: None, 1: None}
                pT_pv = {0: 0, 1: 0}
                pT_T = {0: 0, 1: 0}
                den_of = [0, 0]
                ctxnorm_of = [0, 0]
                n_bclast = 0
                n_recip = 0
                n_bc = 0
                n_ctxnorm = 0
                n_nrm_busy = 0
                pending_tail = [None]
                n_bclast_cell = [0]
                n_recip_cell = [0]
                n_bc_cell = [0]
                n_ctxnorm_cell = [0]

                for g in range(GL if stage != "a" else 0):
                    for qb in range(NQB):
                        qi = g * NQB + qb
                        b.w("sync", "c1", 1)
                        if qi >= 2:
                            b.w("sync", "t", n_qkvmm.get((qi - 2, 2), 0))
                        r, off = qb // 2, (qb % 2) * 512
                        n_rhs[qi] = b.dma(
                            "sync",
                            rhs_sb[qi % 2][:].rearrange("p (c t) -> p c t", t=512),
                            cc1_out[r * H:(r + 1) * H, off:off + 512].rearrange("(c p) t -> p c t", p=P))
                        b.w("tensor", "sx", n_rhs[qi])
                        b.w("tensor", "sg", n_wq)
                        for dcol in range(3):
                            pb = (qi * 3 + dcol) % 2
                            b.wt("tensor", sfree[pb])
                            col0 = (g * 3 + dcol) * P
                            for hc in range(HC):
                                n_mm = b.oi("tensor", lambda e, h=hc, c0=col0, pb=pb, buf=rhs_sb[qi % 2],
                                            first=(hc == 0), last=(hc == HC - 1):
                                            e.matmul(out=psum_s[pb][:, 0:512],
                                                     lhsT=wqkv_sb[:, h * 768 + c0: h * 768 + c0 + P],
                                                     rhs=buf[:, h * 512:(h + 1) * 512],
                                                     start=first, stop=last), "t")
                            n_qkvmm[(qi, dcol)] = n_mm
                            b.w("scalar", "t", n_mm)
                            if dcol == 0:
                                n_ev = b.oi("scalar", lambda e, pb=pb, qb=qb:
                                            e.activation(out=qT_sb[:, qb * 512:(qb + 1) * 512],
                                                         in_=psum_s[pb][:, 0:512], func=AF.Identity,
                                                         bias=0.0, scale=INV_SQRT_HN), "s")
                            elif dcol == 1:
                                n_ev = b.oi("scalar", lambda e, pb=pb, qb=qb:
                                            e.copy(out=kT_sb[:, qb * 512:(qb + 1) * 512],
                                                   in_=psum_s[pb][:, 0:512]), "s")
                            else:
                                if qi >= 2:
                                    b.w("scalar", "t", n_vtr[qi - 2])
                                n_ev = b.oi("scalar", lambda e, pb=pb, vb=vstage[qi % 2]:
                                            e.copy(out=vb[:], in_=psum_s[pb][:, 0:512]), "s")
                                b.w("tensor", "s", n_ev)
                                if qi >= 1:
                                    b.w("tensor", "v", n_vevac[qi - 1])
                                for i in range(4):
                                    n_tr = b.oi("tensor", lambda e, i=i, vb=vstage[qi % 2]:
                                                e.transpose(out=psum_trB[:, i * P:(i + 1) * P],
                                                            in_=vb[:, i * P:(i + 1) * P],
                                                            identity=ident_bf[:]), "t")
                                n_vtr[qi] = n_tr
                                b.w("vector", "t", n_tr)
                                n_vevac[qi] = b.oi("vector", lambda e, qb=qb:
                                                   e.tensor_copy(out=v_sb[:, qb * 512:(qb + 1) * 512],
                                                                 in_=psum_trB[:]), "v")
                            n_evac[(qi, dcol)] = n_ev
                            sfree[pb] = ("s", n_ev)

                    if stage == "qkv" and g == 0:
                        b.w("gpsimd", "v", n_vevac[NQB - 1])
                        b.w("gpsimd", "s", n_evac[(NQB - 1, 1)])
                        for nm, src_ in (("d_qT", qT_sb), ("d_kT", kT_sb), ("d_v", v_sb)):
                            b.dma("gpsimd", dbg[nm][:], src_[:])
                        b.dma("gpsimd", dbg["d_wq"][:], wqkv_sb[:])
                        b.w("gpsimd", "sg", b.ctr["sg"])
                        break

                    b.w("vector", "s", n_evac[(g * NQB + NQB - 1, 1)])
                    for which, src_ in ((0, qT_sb), (1, kT_sb)):
                        first_acc = True
                        for blk in range(4):
                            n_sq = b.oi("vector", lambda e, s_=src_, blk=blk:
                                        e.tensor_mul(out=sq_bf[:], in0=s_[:, blk * H:(blk + 1) * H],
                                                     in1=s_[:, blk * H:(blk + 1) * H]), "v")
                            for j in range(4):
                                b.w("tensor", "v", max(n_sq, n_nrm_busy))
                                b.w("tensor", "s", max(n_bclast, n_bclast_cell[0]))
                                b.w("tensor", "v", max(n_recip, n_recip_cell[0]))
                                n_nmm = b.oi("tensor", lambda e, j=j:
                                             e.matmul(out=psum_bc[0:1, :], lhsT=ones_col_bf[:],
                                                      rhs=sq_bf[:, j * 512:(j + 1) * 512],
                                                      start=True, stop=True), "t")
                                b.w("vector", "t", n_nmm)
                                n_rm11 = b.oi("vector", lambda e: e.reduce_max(out=m11[:], in_=psum_bc[0:1, :],
                                                                               axis=AX.X), "v")
                                b.w("vector", "v", n_rm11)
                                if first_acc:
                                    n_nrm_busy = b.oi("vector", lambda e, w=which:
                                                      e.tensor_copy(out=mx2[w][:], in_=m11[:]), "v")
                                    first_acc = False
                                else:
                                    n_nrm_busy = b.oi("vector", lambda e, w=which:
                                                      e.tensor_tensor(out=mx2[w][:], in0=mx2[w][:],
                                                                      in1=m11[:], op=ALU.max), "v")
                    n_m2 = b.oi("vector", lambda e: e.tensor_mul(out=m11[:], in0=mx2[0][:],
                                                                 in1=mx2[1][:]), "v")
                    b.w("scalar", "v", n_m2)
                    n_bq = b.oi("scalar", lambda e: e.activation(out=bq11[:], in_=m11[:], func=AF.Sqrt,
                                                                 bias=0.0, scale=1.0), "s")
                    b.w("vector", "s", n_bq)
                    n_negb1 = b.oi("vector", lambda e: e.tensor_single_scalar(out=bq11[:], in_=bq11[:],
                                                                              scalar=-1.0, op=ALU.mult), "v")
                    b.w("vector", "v", n_negb1)
                    n_negb1 = b.oi("vector", lambda e: e.tensor_copy(out=m11[:], in_=bq11[:]), "v")
                    b.w("tensor", "v", max(n_negb1, n_ones))
                    b.w("tensor", "s", max(n_bclast, n_bclast_cell[0]))
                    n_nbmm = b.oi("tensor", lambda e: e.matmul(out=psum_bc[:, 0:1],
                                                               lhsT=ones_row_f[:], rhs=bq11[:],
                                                               start=True, stop=True), "t")
                    b.w("scalar", "t", n_nbmm)
                    n_negb = b.oi("scalar", lambda e: e.copy(out=negb_sb[:], in_=psum_bc[:, 0:1]), "s")
                    n_bclast = n_negb
                    n_bclast_cell[0] = n_negb

                    for qb in range(NQB):
                        b.w("tensor", "s", n_negb)
                        n_T = 0
                        n_pv_last = 0
                        for p in range(NKC // 2):
                            if p == 4 and pending_tail[0] is not None:
                                pending_tail[0]()
                                pending_tail[0] = None
                            b_ = p % 2
                            b.wt("tensor", sfree[b_])
                            for half in range(2):
                                kc = 2 * p + half
                                nS = b.oi("tensor", lambda e, kc=kc, qb=qb, b_=b_, half=half:
                                          e.matmul(out=psum_s[b_][:, half * 512:(half + 1) * 512],
                                                   lhsT=kT_sb[:, kc * P:(kc + 1) * P],
                                                   rhs=qT_sb[:, qb * 512:(qb + 1) * 512],
                                                   start=True, stop=True), "t")
                            b.w("scalar", "t", nS)
                            b.w("scalar", "t", pT_pv[b_])
                            b.w("scalar", "v", pT_T[b_])
                            n_exp = b.oi("scalar", lambda e, b_=b_:
                                         e.activation(out=pT_sb[b_][:], in_=psum_s[b_][:], func=AF.Exp,
                                                      bias=negb_sb[:], scale=1.0), "s")
                            sfree[b_] = ("s", n_exp)
                            b.w("tensor", "s", n_exp)
                            if p == 0:
                                b.w("tensor", "v", ctxnorm_of[qb % 2])
                            for half in range(2):
                                kc = 2 * p + half
                                n_pv = b.oi("tensor", lambda e, kc=kc, b_=b_, half=half, qb=qb,
                                            first=(kc == 0), last=(kc == NKC - 1):
                                            e.matmul(out=psum_ctx[qb % 2][:],
                                                     lhsT=v_sb[:, kc * P:(kc + 1) * P],
                                                     rhs=pT_sb[b_][:, half * 512:(half + 1) * 512],
                                                     start=first, stop=last), "t")
                            pT_pv[b_] = n_pv
                            n_pv_last = n_pv
                            b.w("vector", "s", n_exp)
                            n_tmp = b.oi("vector", lambda e, b_=b_: e.tensor_tensor(
                                out=tmp_bf[:], in0=pT_sb[b_][:, 0:512], in1=pT_sb[b_][:, 512:1024],
                                op=ALU.add), "v")
                            pT_T[b_] = n_tmp
                            if p == 0:
                                b.w("vector", "t", den_of[qb % 2])
                                n_T = b.oi("vector", lambda e, qb=qb:
                                           e.tensor_copy(out=Tacc[qb % 2][:], in_=tmp_bf[:]), "v")
                            else:
                                n_T = b.oi("vector", lambda e, qb=qb:
                                           e.tensor_tensor(out=Tacc[qb % 2][:], in0=Tacc[qb % 2][:],
                                                           in1=tmp_bf[:], op=ALU.add), "v")

                        def make_tail(g=g, qb=qb, n_T=n_T, n_pv_last=n_pv_last):
                            def tail():
                                b.w("tensor", "v", n_T)
                                b.w("tensor", "s", n_bclast_cell[0])
                                n_den = b.oi("tensor", lambda e, qb=qb:
                                             e.matmul(out=psum_bc[0:1, :], lhsT=ones_col_f[:],
                                                      rhs=Tacc[qb % 2][:], start=True, stop=True), "t")
                                den_of[qb % 2] = n_den
                                b.w("vector", "t", max(n_den, n_bc_cell[0]))
                                n_rec = b.oi("vector", lambda e: e.reciprocal(out=invden[:],
                                                                              in_=psum_bc[0:1, :]), "v")
                                n_recip_cell[0] = n_rec
                                b.w("tensor", "v", n_rec)
                                n_bc_ = b.oi("tensor", lambda e: e.matmul(out=psum_bc[:],
                                                                          lhsT=ones_row_f[:], rhs=invden[:],
                                                                          start=True, stop=True), "t")
                                n_bc_cell[0] = n_bc_
                                b.w("scalar", "t", n_bc_)
                                n_bcv = b.oi("scalar", lambda e, qb=qb:
                                             e.copy(out=bcast_sb[qb % 2][:], in_=psum_bc[:]), "s")
                                n_bclast_cell[0] = n_bcv
                                b.w("vector", "s", n_bcv)
                                b.w("vector", "t", n_pv_last)
                                n_cn = b.oi("vector", lambda e, g=g, qb=qb: e.tensor_tensor(
                                    out=ctxT_sb[:, g * SQ + qb * 512: g * SQ + (qb + 1) * 512],
                                    in0=psum_ctx[qb % 2][:], in1=bcast_sb[qb % 2][:], op=ALU.mult), "v")
                                ctxnorm_of[qb % 2] = n_cn
                                n_ctxnorm_cell[0] = n_cn
                            return tail
                        pending_tail[0] = make_tail()
                    if pending_tail[0] is not None:
                        pending_tail[0]()
                        pending_tail[0] = None
                    n_bclast = n_bclast_cell[0]
                    n_recip = n_recip_cell[0]
                    n_bc = n_bc_cell[0]
                    n_ctxnorm = n_ctxnorm_cell[0]

                if stage not in ("qkv", "a"):
                    b.w("gpsimd", "v", n_ctxnorm)
                    for g in range(GL):
                        b.dma("gpsimd", cc2_in[g * P:(g + 1) * P, :], ctxT_sb[:, g * SQ:(g + 1) * SQ])
                    b.w("gpsimd", "sg", b.ctr["sg"])
                    b.oc("gpsimd", lambda e: e.collective_compute(
                        "AllGather", ALU.bypass, replica_groups=RG,
                        ins=[cc2_in[:]], outs=[cc2_out[:]]), "c2")

        if stage in ("qkv", "a"):
            pass
        elif stage == "ctx":
            tmp = sb("dbgctx", [P, GL * SQ], BF16)
            b.w("sync", "c2", 1)
            for g in range(GL):
                b.dma("sync", tmp[:, g * SQ:(g + 1) * SQ], cc2_in[g * P:(g + 1) * P, :])
            b.w("gpsimd", "sx", b.ctr["sx"])
            for g in range(GL):
                b.dma("gpsimd", dbg["d_ctx"][g * P:(g + 1) * P, :], tmp[:, g * SQ:(g + 1) * SQ])
            b.w("gpsimd", "sg", b.ctr["sg"])
        else:
            with contextlib.ExitStack() as stkD:
                sbD = lambda name, shape, dt: stkD.enter_context(nc.sbuf_tensor(name, shape, dt))
                psD = lambda name, shape, dt: stkD.enter_context(nc.psum_tensor(name, shape, dt))
                x_sb = sbD("x_sb", [P, NT * H], F32)
                psum_d = [psD(f"psd{i}", [P, 512], F32) for i in range(2)]
                psum_bc_d = psD("psbcd", [P, 512], F32)
                psum_trD = psD("pstrD", [P, 512], BF16)

                psd_free = {0: None, 1: None}
                d1scope = contextlib.ExitStack()
                with d1scope:
                    sb1 = lambda name, shape, dt: d1scope.enter_context(nc.sbuf_tensor(name, shape, dt))
                    ctx_sb = sb1("ctx_sb", [P, HC * TOK], BF16)
                    wproj_sb = [sb1(f"wproj{i}", [P, HC * 512], BF16) for i in range(2)]

                    b.w("sync", "c2", 1)
                    n_xl = b.dma("sync", x_sb[:].rearrange("p (c d) -> p c d", d=H),
                                 xs[:].rearrange("(c p) d -> p c d", p=P))
                    n_ctxl = 0
                    for cc in range(HC):
                        n_ctxl = b.oi("sync", lambda e, cc=cc: e.dma_start(
                            out=ctx_sb[:, cc * TOK:(cc + 1) * TOK],
                            in_=cc2_out[cc * P:(cc + 1) * P, bass.ts(pid_cell["v"], TOK)]), "sx", 16)

                    b.w("tensor", "c2", 1)
                    n_bse = 0
                    for o4 in range(4):
                        b.w("tensor", "s", n_bse)
                        n_bmm = b.oi("tensor", lambda e, o4=o4: e.matmul(
                            out=psum_bc_d[:], lhsT=ones_row_f[:],
                            rhs=bprojrow[0:1, o4 * 512:(o4 + 1) * 512], start=True, stop=True), "t")
                        b.w("scalar", "t", n_bmm)
                        n_bse = b.oi("scalar", lambda e, o4=o4: e.copy(
                            out=b_sb[:, o4 * 512:(o4 + 1) * 512], in_=psum_bc_d[:]), "s")
                    n_xb = [0] * NT
                    for tt in range(NT):
                        b.w("vector", "sx", n_xl)
                        b.w("vector", "s", n_bse)
                        n_xb[tt] = b.oi("vector", lambda e, tt=tt: e.tensor_tensor(
                            out=x_sb[:, tt * H:(tt + 1) * H], in0=x_sb[:, tt * H:(tt + 1) * H],
                            in1=b_sb[:], op=ALU.add), "v")

                    b.w("gpsimd", "c2", 1)
                    n_d1 = {}
                    for ob in range(4):
                        if ob >= 2:
                            b.w("gpsimd", "t", n_d1.get((ob - 2, NT - 1), 0))
                        n_wl = b.dma("gpsimd", wproj_sb[ob % 2][:].rearrange("p (c o) -> p c o", o=512),
                                     wprojT[:, ob * 512:(ob + 1) * 512].rearrange("(c p) o -> p c o", p=P))
                        b.w("tensor", "sx", n_ctxl)
                        b.w("tensor", "sg", n_wl)
                        for tt in range(NT):
                            pb = (ob * NT + tt) % 2
                            b.wt("tensor", psd_free[pb])
                            for cc in range(HC):
                                n_mm = b.oi("tensor", lambda e, cc=cc, tt=tt, pb=pb, wb=wproj_sb[ob % 2],
                                            first=(cc == 0), last=(cc == HC - 1):
                                            e.matmul(out=psum_d[pb][:],
                                                     lhsT=ctx_sb[:, cc * TOK + tt * P: cc * TOK + (tt + 1) * P],
                                                     rhs=wb[:, cc * 512:(cc + 1) * 512],
                                                     start=first, stop=last), "t")
                            n_d1[(ob, tt)] = n_mm
                            b.w("vector", "t", n_mm)
                            b.w("vector", "v", n_xb[tt])
                            n_ev = b.oi("vector", lambda e, tt=tt, ob=ob, pb=pb: e.tensor_tensor(
                                out=x_sb[:, tt * H + ob * 512: tt * H + (ob + 1) * 512],
                                in0=psum_d[pb][:],
                                in1=x_sb[:, tt * H + ob * 512: tt * H + (ob + 1) * 512],
                                op=ALU.add), "v")
                            psd_free[pb] = ("v", n_ev)

                ln2T_sb = sbD("ln2T", [P, HC * TOK], BF16)
                h1T_sb = sbD("h1T", [P, HC * TOK], BF16)
                alphaE_sb = sbD("alphaE", [P, 4 * TOK], BF16)
                avxT_sb = sbD("avxT", [P, 4 * TOK], BF16)
                uflat_sb = sbD("uflat_b", [P, 4 * H], BF16)

                n_ln2 = [0] * NT
                n_tr2last = [0] * NT
                n_tr2evac = 0
                with contextlib.ExitStack() as stk2:
                    sb2 = lambda name, shape, dt: stk2.enter_context(nc.sbuf_tensor(name, shape, dt))
                    ln2_bf = [sb2(f"ln2bf{i}", [P, H], BF16) for i in range(2)]
                    sq_scrD = sb2("sq_scrD", [P, H], F32)
                    ssumD = sb2("ssumD", [P, 1], F32)
                    negmeanD = [sb2(f"negmeanD{i}", [P, 1], F32) for i in range(2)]
                    varsumD = [sb2(f"varsumD{i}", [P, 1], F32) for i in range(2)]
                    stdvD = [sb2(f"stdvD{i}", [P, 1], F32) for i in range(2)]
                    rstdD = [sb2(f"rstdD{i}", [P, 1], F32) for i in range(2)]
                    nmrD = [sb2(f"nmrD{i}", [P, 1], F32) for i in range(2)]
                    for tt in range(NT):
                        if tt >= 2:
                            b.w("vector", "s", n_ln2[tt - 2])
                        n_redD = b.oi("vector", lambda e, tt=tt: e.reduce_sum(
                            out=ssumD[:], in_=x_sb[:, tt * H:(tt + 1) * H], axis=AX.X), "v")
                        b.w("vector", "v", n_redD)
                        n_nm = b.oi("vector", lambda e, tt=tt: e.tensor_single_scalar(
                            out=negmeanD[tt % 2][:], in_=ssumD[:], scalar=-1.0 / H, op=ALU.mult), "v")
                        b.w("scalar", "v", n_nm)
                        if tt >= 2:
                            b.w("scalar", "t", n_tr2last[tt - 2])
                        n_sqaD = b.oi("scalar", lambda e, tt=tt: e.activation(
                            out=sq_scrD[:], in_=x_sb[:, tt * H:(tt + 1) * H], func=AF.Square,
                            bias=negmeanD[tt % 2][:], scale=1.0, accum_out=varsumD[tt % 2][:]), "s")
                        b.w("scalar", "s", n_sqaD)
                        n_std = b.oi("scalar", lambda e, tt=tt: e.activation(
                            out=stdvD[tt % 2][:], in_=varsumD[tt % 2][:], func=AF.Sqrt,
                            bias=eps_sb[:], scale=1.0 / H), "s")
                        b.w("vector", "s", n_std)
                        n_recD = b.oi("vector", lambda e, tt=tt: e.reciprocal(out=rstdD[tt % 2][:],
                                                                              in_=stdvD[tt % 2][:]), "v")
                        b.w("vector", "v", n_recD)
                        n_nmr = b.oi("vector", lambda e, tt=tt: e.tensor_mul(
                            out=nmrD[tt % 2][:], in0=negmeanD[tt % 2][:], in1=rstdD[tt % 2][:]), "v")
                        b.w("scalar", "v", n_nmr)
                        n_ln2[tt] = b.oi("scalar", lambda e, tt=tt: e.activation(
                            out=ln2_bf[tt % 2][:], in_=x_sb[:, tt * H:(tt + 1) * H], func=AF.Identity,
                            bias=nmrD[tt % 2][:], scale=rstdD[tt % 2][:]), "s")
                        b.w("tensor", "s", n_ln2[tt])
                        for grp in range(4):
                            b.w("tensor", "v", n_tr2evac)
                            for i in range(4):
                                hc = grp * 4 + i
                                n_tr2last[tt] = b.oi("tensor", lambda e, tt=tt, hc=hc, i=i:
                                                     e.transpose(out=psum_trD[:, i * P:(i + 1) * P],
                                                                 in_=ln2_bf[tt % 2][:, hc * P:(hc + 1) * P],
                                                                 identity=ident_bf[:]), "t")
                            b.w("vector", "t", n_tr2last[tt])
                            n_tr2evac = b.oi("vector", lambda e, tt=tt, grp=grp: e.tensor_copy(
                                out=ln2T_sb[:].rearrange("p (c t) -> p c t", t=TOK)
                                    [:, grp * 4:(grp + 1) * 4, tt * P:(tt + 1) * P],
                                in_=psum_trD[:].rearrange("p (i t) -> p i t", t=P)), "v")

                n_h1 = {}
                with contextlib.ExitStack() as stk3:
                    sb3 = lambda name, shape, dt: stk3.enter_context(nc.sbuf_tensor(name, shape, dt))
                    w1_sb = [sb3(f"w1sb{i}", [P, H], BF16) for i in range(2)]
                    b.w("gpsimd", "t", n_tr2last[NT - 1])
                    for ec in range(HC):
                        if ec >= 2:
                            b.w("gpsimd", "t", n_h1.get((ec - 2, 1), 0))
                        n_wl = b.dma("gpsimd", w1_sb[ec % 2][:].rearrange("p (c e) -> p c e", e=P),
                                     w1[:, ec * P:(ec + 1) * P].rearrange("(c p) e -> p c e", p=P))
                        b.w("tensor", "sg", n_wl)
                        b.w("tensor", "v", n_tr2evac)
                        for tb in range(2):
                            pb = (ec * 2 + tb) % 2
                            b.wt("tensor", psd_free[pb])
                            for dc in range(HC):
                                n_mm = b.oi("tensor", lambda e, dc=dc, tb=tb, pb=pb, ec=ec,
                                            first=(dc == 0), last=(dc == HC - 1):
                                            e.matmul(out=psum_d[pb][:],
                                                     lhsT=w1_sb[ec % 2][:, dc * P:(dc + 1) * P],
                                                     rhs=ln2T_sb[:, dc * TOK + tb * 512:
                                                                 dc * TOK + (tb + 1) * 512],
                                                     start=first, stop=last), "t")
                            n_h1[(ec, tb)] = n_mm
                            b.w("scalar", "t", n_mm)
                            n_ev = b.oi("scalar", lambda e, ec=ec, tb=tb, pb=pb: e.activation(
                                out=h1T_sb[:, ec * TOK + tb * 512: ec * TOK + (tb + 1) * 512],
                                in_=psum_d[pb][:], func=AF.Relu, bias=b1c_sb[:, ec:ec + 1],
                                scale=1.0), "s")
                            psd_free[pb] = ("s", n_ev)

                n_atr = 0
                n_aevac = n_tr2evac
                with contextlib.ExitStack() as stk4:
                    sb4 = lambda name, shape, dt: stk4.enter_context(nc.sbuf_tensor(name, shape, dt))
                    l_sb = sb4("l_sb", [P, 8], F32)
                    negmaxA = sb4("negmaxA", [P, 1], F32)
                    e8 = sb4("e8", [P, 8], F32)
                    denA = sb4("denA", [P, 1], F32)
                    invA = sb4("invA", [P, 1], F32)
                    alpha_bf = sb4("alpha_bf", [P, 8], BF16)
                    b.w("tensor", "sg", n_w2l)
                    b.w("tensor", "s", n_bse)
                    n_b2mm = b.oi("tensor", lambda e: e.matmul(out=psum_bc_d[:, 0:8], lhsT=ones_row_f[:],
                                                               rhs=b2row[:], start=True, stop=True), "t")
                    b.w("scalar", "t", n_b2mm)
                    n_b2e = b.oi("scalar", lambda e: e.copy(out=b2b_sb[:], in_=psum_bc_d[:, 0:8]), "s")
                    n_e8 = 0
                    n_ab = 0
                    n_recA = 0
                    n_atr_d4 = n_tr2last[NT - 1]
                    for tt in range(NT):
                        pb = tt % 2
                        b.wt("tensor", psd_free[pb])
                        for ec in range(HC):
                            n_mm = b.oi("tensor", lambda e, ec=ec, tt=tt, pb=pb,
                                        first=(ec == 0), last=(ec == HC - 1):
                                        e.matmul(out=psum_d[pb][:, 0:8],
                                                 lhsT=h1T_sb[:, ec * TOK + tt * P: ec * TOK + (tt + 1) * P],
                                                 rhs=w2_sb[:, ec * 8:(ec + 1) * 8],
                                                 start=first, stop=last), "t")
                        b.w("vector", "t", n_mm)
                        b.w("vector", "s", max(n_b2e, n_e8))
                        n_l = b.oi("vector", lambda e, pb=pb: e.tensor_tensor(
                            out=l_sb[:], in0=psum_d[pb][:, 0:8], in1=b2b_sb[:], op=ALU.add), "v")
                        psd_free[pb] = ("v", n_l)
                        n_rm = b.oi("vector", lambda e: e.reduce_max(out=negmaxA[:], in_=l_sb[:],
                                                                     axis=AX.X, negate=True), "v")
                        b.w("scalar", "v", max(n_rm, n_recA, n_ab))
                        n_e8 = b.oi("scalar", lambda e: e.activation(out=e8[:], in_=l_sb[:], func=AF.Exp,
                                                                     bias=negmaxA[:], scale=1.0,
                                                                     accum_out=denA[:]), "s")
                        b.w("vector", "s", n_e8)
                        n_recA = b.oi("vector", lambda e: e.reciprocal(out=invA[:], in_=denA[:]), "v")
                        b.w("vector", "v", n_recA)
                        b.w("vector", "t", n_atr)
                        n_ab = b.oi("vector", lambda e: e.tensor_scalar_mul(out=alpha_bf[:], in0=e8[:],
                                                                            scalar1=invA[:]), "v")
                        b.w("tensor", "v", max(n_ab, n_aevac))
                        b.w("tensor", "t", n_atr_d4)
                        n_atr = b.oi("tensor", lambda e: e.transpose(out=psum_trD[0:8, 0:P],
                                                                     in_=alpha_bf[:],
                                                                     identity=ident_bf[:]), "t")
                        n_atr_d4 = n_atr
                        b.w("vector", "t", n_atr)
                        n_aevac = b.oi("vector", lambda e, tt=tt: e.tensor_copy(
                            out=alphaT_sb[:, tt * P:(tt + 1) * P], in_=psum_trD[0:8, 0:P]), "v")

                n_aE = n_b2e
                for krc in range(4):
                    for tb in range(2):
                        b.w("tensor", "v", n_aevac)
                        b.w("tensor", "s", n_aE)
                        n_mm = b.oi("tensor", lambda e, krc=krc, tb=tb: e.matmul(
                            out=psum_bc_d[:], lhsT=sel_sb[:, krc * P:(krc + 1) * P],
                            rhs=alphaT_sb[:, tb * 512:(tb + 1) * 512], start=True, stop=True), "t")
                        b.w("scalar", "t", n_mm)
                        n_aE = b.oi("scalar", lambda e, krc=krc, tb=tb: e.copy(
                            out=alphaE_sb[:, krc * TOK + tb * 512: krc * TOK + (tb + 1) * 512],
                            in_=psum_bc_d[:]), "s")

                n_avx = 0
                with contextlib.ExitStack() as stk6:
                    sb6 = lambda name, shape, dt: stk6.enter_context(nc.sbuf_tensor(name, shape, dt))
                    vflat_sb = sb6("vflat", [P, HC * 512], BF16)
                    b.w("gpsimd", "t", n_atr)
                    b.w("gpsimd", "s", n_aE)
                    n_vfl = b.dma("gpsimd", vflat_sb[:].rearrange("p (c k) -> p c k", k=512),
                                  vflatT[:].rearrange("(c p) k -> p c k", p=P))
                    n_ufl = b.dma("gpsimd", uflat_sb[:].rearrange("p (c d) -> p c d", d=H),
                                  uflat[:].rearrange("(c p) d -> p c d", p=P))

                    for krc in range(4):
                        for tb in range(2):
                            pb = (krc * 2 + tb) % 2
                            b.w("tensor", "sg", n_vfl)
                            b.wt("tensor", psd_free[pb])
                            for dc in range(HC):
                                n_mm = b.oi("tensor", lambda e, dc=dc, krc=krc, tb=tb, pb=pb,
                                            first=(dc == 0), last=(dc == HC - 1):
                                            e.matmul(out=psum_d[pb][:],
                                                     lhsT=vflat_sb[:, dc * 512 + krc * P:
                                                                   dc * 512 + (krc + 1) * P],
                                                     rhs=ln2T_sb[:, dc * TOK + tb * 512:
                                                                 dc * TOK + (tb + 1) * 512],
                                                     start=first, stop=last), "t")
                            b.w("vector", "t", n_mm)
                            b.w("vector", "s", n_aE)
                            n_avx = b.oi("vector", lambda e, krc=krc, tb=tb, pb=pb: e.tensor_tensor(
                                out=avxT_sb[:, krc * TOK + tb * 512: krc * TOK + (tb + 1) * 512],
                                in0=psum_d[pb][:],
                                in1=alphaE_sb[:, krc * TOK + tb * 512: krc * TOK + (tb + 1) * 512],
                                op=ALU.mult), "v")
                            psd_free[pb] = ("v", n_avx)

                with contextlib.ExitStack() as stk7:
                    sb7 = lambda name, shape, dt: stk7.enter_context(nc.sbuf_tensor(name, shape, dt))
                    out_sb = [sb7(f"outsb{i}", [P, H], F32) for i in range(2)]
                    n_store = [0] * NT
                    for tt in range(NT):
                        n_ev = 0
                        for ob in range(4):
                            pb = (tt * 4 + ob) % 2
                            b.w("tensor", "sg", n_ufl)
                            b.w("tensor", "v", n_avx)
                            b.wt("tensor", psd_free[pb])
                            for krc in range(4):
                                n_mm = b.oi("tensor", lambda e, krc=krc, tt=tt, ob=ob, pb=pb,
                                            first=(krc == 0), last=(krc == 3):
                                            e.matmul(out=psum_d[pb][:],
                                                     lhsT=avxT_sb[:, krc * TOK + tt * P:
                                                                  krc * TOK + (tt + 1) * P],
                                                     rhs=uflat_sb[:, krc * H + ob * 512:
                                                                  krc * H + (ob + 1) * 512],
                                                     start=first, stop=last), "t")
                            b.w("vector", "t", n_mm)
                            if tt >= 2 and ob == 0:
                                b.w("vector", "sg", n_store[tt - 2])
                            n_ev = b.oi("vector", lambda e, tt=tt, ob=ob, pb=pb: e.tensor_tensor(
                                out=out_sb[tt % 2][:, ob * 512:(ob + 1) * 512],
                                in0=psum_d[pb][:],
                                in1=x_sb[:, tt * H + ob * 512: tt * H + (ob + 1) * 512],
                                op=ALU.add), "v")
                            psd_free[pb] = ("v", n_ev)
                        b.w("gpsimd", "v", n_ev)
                        n_store[tt] = b.dma("gpsimd", out[tt * P:(tt + 1) * P, :], out_sb[tt % 2][:])
                    b.w("gpsimd", "sg", n_store[NT - 1])

        with nc.Block() as block:
            @block.sync
            def _(eng):
                for fn in b.prog["sync"]:
                    fn(eng)

            @block.tensor
            def _(eng):
                for fn in b.prog["tensor"]:
                    fn(eng)

            @block.scalar
            def _(eng):
                for fn in b.prog["scalar"]:
                    fn(eng)

            @block.vector
            def _(eng):
                for fn in b.prog["vector"]:
                    fn(eng)

            @block.gpsimd
            def _(eng):
                for fn in b.prog["gpsimd"]:
                    fn(eng)

    return nc


def _host_inputs(inputs):
    f32 = np.float32
    q = np.asarray(inputs["L0_attn_query"], dtype=f32).reshape(SQ, H)
    Wqkv = np.asarray(inputs["L0_attn_Wqkv"], dtype=f32)
    Wproj = np.asarray(inputs["L0_attn_Wproj"], dtype=f32)
    bproj = np.asarray(inputs["L0_attn_bproj"], dtype=f32).reshape(1, H)
    U = np.asarray(inputs["L0_ffn_U"], dtype=f32)
    V = np.asarray(inputs["L0_ffn_V"], dtype=f32)
    W1 = np.asarray(inputs["L0_ffn_W1"], dtype=f32)
    b1 = np.asarray(inputs["L0_ffn_b1"], dtype=f32)
    W2 = np.asarray(inputs["L0_ffn_W2"], dtype=f32)
    b2 = np.asarray(inputs["L0_ffn_b2"], dtype=f32).reshape(1, 8)

    WqkvT = np.ascontiguousarray(Wqkv.T)
    WprojT = np.ascontiguousarray(Wproj.T)
    b1c = np.ascontiguousarray(b1.reshape(HC, P).T)
    VflatT = np.ascontiguousarray(V.reshape(512, H).T)
    Uflat = np.ascontiguousarray(U.transpose(0, 2, 1).reshape(512, H))
    sel = np.zeros((8, 512), dtype=f32)
    for k in range(8):
        sel[k, k * 64:(k + 1) * 64] = 1.0
    identity = np.eye(P, dtype=f32)

    in_maps = []
    for c in range(8):
        in_maps.append({
            "xs": np.ascontiguousarray(q[c * TOK:(c + 1) * TOK, :]),
            "wqkvT": np.ascontiguousarray(WqkvT[:, c * GL * 3 * P:(c + 1) * GL * 3 * P]),
            "wprojT": WprojT,
            "bproj": bproj,
            "w1": W1,
            "b1c": b1c,
            "w2": W2,
            "b2": b2,
            "vflatT": VflatT,
            "uflat": Uflat,
            "sel": sel,
            "ident": identity,
        })
    return in_maps


def kernel(**inputs):
    inputs.pop("L0_attn_key", None)
    inputs.pop("L0_attn_value", None)
    nc = build("full")
    in_maps = _host_inputs(inputs)
    res = run_bass_kernel_spmd(nc, in_maps, core_ids=list(range(8)))
    full = np.concatenate([res.results[c]["out"] for c in range(8)], axis=0)
    return full.reshape(1, SQ, H)


# version 06
# speedup vs baseline: 1.6012x; 1.6012x over previous
import contextlib
import numpy as np
import concourse.bass as bass
import concourse.mybir as mybir
from concourse.bass_utils import run_bass_kernel_spmd

F32 = mybir.dt.float32
BF16 = mybir.dt.bfloat16
AF = mybir.ActivationFunctionType
ALU = mybir.AluOpType
AX = mybir.AxisListType
ET = mybir.EngineType

P = 128
SQ = 8192
H = 2048
TOK = 1024
NT = TOK // P
HC = H // P
NQB = SQ // 512
NKC = SQ // P
GL = 2
RG = [[0, 1, 2, 3, 4, 5, 6, 7]]
INV_SQRT_HN = 0.08838834764831845
EPS = 1e-5

ENGINES = ("sync", "tensor", "scalar", "vector", "gpsimd")


class B:

    def __init__(self, nc, stk):
        self.nc = nc
        self.prog = {e: [] for e in ENGINES}
        self.sem = {}
        self.ctr = {}
        self.waited = {e: {} for e in ENGINES}
        for name in ("sx", "sg", "c1", "c2", "t", "s", "v"):
            self.sem[name] = stk.enter_context(nc.semaphore(name))
            self.ctr[name] = 0

    def w(self, e, s, val):
        if not val or val <= 0:
            return
        if self.waited[e].get(s, 0) >= val:
            return
        self.waited[e][s] = val
        h = self.sem[s]
        self.prog[e].append(lambda eng, h=h, val=val: eng.wait_ge(h, val))

    def wt(self, e, tag):
        if tag:
            self.w(e, tag[0], tag[1])

    def o(self, e, fn):
        self.prog[e].append(fn)

    def oi(self, e, fn, s, k=1):
        self.ctr[s] += k
        n = self.ctr[s]
        h = self.sem[s]
        self.prog[e].append(lambda eng, fn=fn, h=h, k=k: fn(eng).then_inc(h, k))
        return n

    def oc(self, e, fn, s):
        self.ctr[s] = max(self.ctr[s], 1)
        h = self.sem[s]
        self.prog[e].append(lambda eng, fn=fn, h=h: fn(eng).then_inc(h))
        return 1

    def dma(self, e, out, in_, s=None):
        if s is None:
            s = "sg" if e == "gpsimd" else "sx"
        return self.oi(e, lambda eng, out=out, in_=in_: eng.dma_start(out=out, in_=in_), s, 16)


def build(stage="full"):
    nc = bass.Bass("TRN2", target_bir_lowering=False, num_devices=8)

    xs = nc.dram_tensor("xs", [TOK, H], F32, kind="ExternalInput").ap()
    wqkvT = nc.dram_tensor("wqkvT", [H, GL * 3 * P], F32, kind="ExternalInput").ap()
    wprojT = nc.dram_tensor("wprojT", [H, H], F32, kind="ExternalInput").ap()
    bproj = nc.dram_tensor("bproj", [1, H], F32, kind="ExternalInput").ap()
    w1 = nc.dram_tensor("w1", [H, H], F32, kind="ExternalInput").ap()
    b1c = nc.dram_tensor("b1c", [P, HC], F32, kind="ExternalInput").ap()
    w2 = nc.dram_tensor("w2", [H, 8], F32, kind="ExternalInput").ap()
    b2 = nc.dram_tensor("b2", [1, 8], F32, kind="ExternalInput").ap()
    vflatT = nc.dram_tensor("vflatT", [H, 512], F32, kind="ExternalInput").ap()
    uflat = nc.dram_tensor("uflat", [512, H], F32, kind="ExternalInput").ap()
    sel = nc.dram_tensor("sel", [8, 512], F32, kind="ExternalInput").ap()
    ident = nc.dram_tensor("ident", [P, P], F32, kind="ExternalInput").ap()
    out = nc.dram_tensor("out", [TOK, H], F32, kind="ExternalOutput").ap()

    cc1_in = nc.dram_tensor("cc1_in", [H, TOK], BF16, kind="Internal").ap()
    cc1_out = nc.dram_tensor("cc1_out", [8 * H, TOK], BF16, kind="Internal").ap()
    cc2_in = nc.dram_tensor("cc2_in", [GL * P, SQ], BF16, kind="Internal").ap()
    cc2_out = nc.dram_tensor("cc2_out", [8 * GL * P, SQ], BF16, kind="Internal").ap()

    dbg = {}
    if stage == "qkv":
        for nm in ("d_qT", "d_kT", "d_v"):
            dbg[nm] = nc.dram_tensor(nm, [P, SQ], F32, kind="ExternalOutput").ap()
        dbg["d_wq"] = nc.dram_tensor("d_wq", [P, HC * GL * 3 * P], F32, kind="ExternalOutput").ap()
    if stage == "ctx":
        dbg["d_ctx"] = nc.dram_tensor("d_ctx", [GL * P, SQ], F32, kind="ExternalOutput").ap()
    if stage == "a":
        dbg["d_ln1T"] = nc.dram_tensor("d_ln1T", [H, TOK], F32, kind="ExternalOutput").ap()

    stk = contextlib.ExitStack()
    with stk:
        b = B(nc, stk)
        sb = lambda name, shape, dt: stk.enter_context(nc.sbuf_tensor(name, shape, dt))

        ident_bf = sb("ident_bf", [P, P], BF16)
        ones_col_f = sb("ones_col_f", [P, 1], F32)
        ones_col_bf = sb("ones_col_bf", [P, 1], BF16)
        ones_row_f = sb("ones_row_f", [1, P], F32)
        bprojrow = sb("bprojrow", [1, H], F32)
        b_sb = sb("b_sb", [P, H], F32)
        b1c_sb = sb("b1c_sb", [P, HC], F32)
        w2_sb = sb("w2_sb", [P, HC * 8], BF16)
        b2row = sb("b2row", [1, 8], F32)
        b2b_sb = sb("b2b_sb", [P, 8], F32)
        sel_sb = sb("sel_sb", [8, 512], BF16)
        alphaT_sb = sb("alphaT_sb", [8, TOK], BF16)
        eps_sb = sb("eps_sb", [P, 1], F32)

        n_id = b.dma("gpsimd", ident_bf[:], ident[:])
        b.dma("sync", b1c_sb[:], b1c[:])
        b.dma("sync", bprojrow[:], bproj[:])
        b.dma("sync", b2row[:], b2[:])
        b.dma("gpsimd", sel_sb[:], sel[:])
        n_w2l = b.dma("gpsimd", w2_sb[:].rearrange("p (c k) -> p c k", k=8),
                      w2[:].rearrange("(c p) k -> p c k", p=P))
        b.oi("vector", lambda e: e.memset(ones_col_f[:], 1.0), "v")
        b.oi("vector", lambda e: e.memset(ones_col_bf[:], 1.0), "v")
        n_ones = b.oi("vector", lambda e: e.memset(ones_row_f[:], 1.0), "v")
        b.oi("vector", lambda e: e.memset(eps_sb[:], EPS), "v")

        pid_cell = {}
        def _getpid(eng):
            pid_cell["v"] = nc.partition_id(engines=[ET.SP])
        b.o("sync", _getpid)

        with contextlib.ExitStack() as stkW:
            sbW = lambda name, shape, dt: stkW.enter_context(nc.sbuf_tensor(name, shape, dt))
            wqkv_sb = sbW("wqkv_sb", [P, HC * GL * 3 * P], BF16)
            n_wq = b.dma("gpsimd", wqkv_sb[:].rearrange("p (c f) -> p c f", f=GL * 3 * P),
                         wqkvT[:].rearrange("(c p) f -> p c f", p=P))

            with contextlib.ExitStack() as stkA:
                sbA = lambda name, shape, dt: stkA.enter_context(nc.sbuf_tensor(name, shape, dt))
                psA = lambda name, shape, dt: stkA.enter_context(nc.psum_tensor(name, shape, dt))
                xtile = [sbA(f"xtile{i}", [P, H], F32) for i in range(2)]
                ln1_bf = [sbA(f"ln1bf{i}", [P, H], BF16) for i in range(2)]
                sq_scr = sbA("sq_scr", [P, H], F32)
                ln1T_sb = sbA("ln1T", [P, HC * TOK], BF16)
                ssum = sbA("ssum", [P, 1], F32)
                negmean = [sbA(f"negmean{i}", [P, 1], F32) for i in range(2)]
                varsum = [sbA(f"varsum{i}", [P, 1], F32) for i in range(2)]
                stdv = [sbA(f"stdv{i}", [P, 1], F32) for i in range(2)]
                rstd = [sbA(f"rstd{i}", [P, 1], F32) for i in range(2)]
                nmr = [sbA(f"nmr{i}", [P, 1], F32) for i in range(2)]
                psum_trA = [psA(f"pstrA{i}", [P, P], BF16) for i in range(2)]

                n_xload = [0] * NT
                n_ln1 = [0] * NT
                n_trlast = [0] * NT
                trA_free = {0: 0, 1: 0}
                last_evac = 0
                for tt in range(NT):
                    if tt >= 2:
                        b.w("sync", "s", n_ln1[tt - 2])
                    n_xload[tt] = b.dma("sync", xtile[tt % 2][:], xs[tt * P:(tt + 1) * P, :])

                    if tt >= 2:
                        b.w("vector", "s", n_ln1[tt - 2])
                    b.w("vector", "sx", n_xload[tt])
                    n_red = b.oi("vector", lambda e, x=xtile[tt % 2]:
                                 e.reduce_sum(out=ssum[:], in_=x[:], axis=AX.X), "v")
                    b.w("vector", "v", n_red)
                    n_nm = b.oi("vector", lambda e, nm=negmean[tt % 2]:
                                e.tensor_single_scalar(out=nm[:], in_=ssum[:], scalar=-1.0 / H,
                                                       op=ALU.mult), "v")

                    b.w("scalar", "v", n_nm)
                    if tt >= 2:
                        b.w("scalar", "t", n_trlast[tt - 2])
                    n_sqa = b.oi("scalar", lambda e, x=xtile[tt % 2], nm=negmean[tt % 2], vs=varsum[tt % 2]:
                                 e.activation(out=sq_scr[:], in_=x[:], func=AF.Square, bias=nm[:], scale=1.0,
                                              accum_out=vs[:]), "s")
                    b.w("scalar", "s", n_sqa)
                    n_std = b.oi("scalar", lambda e, vs=varsum[tt % 2], sd=stdv[tt % 2]:
                                 e.activation(out=sd[:], in_=vs[:], func=AF.Sqrt, bias=eps_sb[:],
                                              scale=1.0 / H), "s")

                    b.w("vector", "s", n_std)
                    n_rec = b.oi("vector", lambda e, sd=stdv[tt % 2], rs=rstd[tt % 2]:
                                 e.reciprocal(out=rs[:], in_=sd[:]), "v")
                    b.w("vector", "v", n_rec)
                    n_nmr = b.oi("vector", lambda e, nm=negmean[tt % 2], rs=rstd[tt % 2], m=nmr[tt % 2]:
                                 e.tensor_mul(out=m[:], in0=nm[:], in1=rs[:]), "v")

                    b.w("scalar", "v", n_nmr)
                    n_ln1[tt] = b.oi("scalar", lambda e, x=xtile[tt % 2], rs=rstd[tt % 2],
                                     m=nmr[tt % 2], o=ln1_bf[tt % 2]:
                                     e.activation(out=o[:], in_=x[:], func=AF.Identity,
                                                  bias=m[:], scale=rs[:]), "s")

                    b.w("tensor", "s", n_ln1[tt])
                    b.w("tensor", "sg", n_id)
                    for hc in range(HC):
                        pb = hc % 2
                        b.w("tensor", "v", trA_free[pb])
                        n_trlast[tt] = b.oi("tensor", lambda e, i=ln1_bf[tt % 2], h=hc, o=psum_trA[pb]:
                                            e.transpose(out=o[:], in_=i[:, h * P:(h + 1) * P],
                                                        identity=ident_bf[:]), "t")
                        b.w("vector", "t", n_trlast[tt])
                        last_evac = b.oi("vector", lambda e, h=hc, t_=tt, i=psum_trA[pb]:
                                         e.tensor_copy(out=ln1T_sb[:, h * TOK + t_ * P:
                                                                   h * TOK + (t_ + 1) * P],
                                                       in_=i[:]), "v")
                        trA_free[pb] = last_evac

                b.w("gpsimd", "v", last_evac)
                n_cc1in = b.dma("gpsimd",
                                cc1_in[:].rearrange("(c p) t -> p c t", p=P),
                                ln1T_sb[:].rearrange("p (c t) -> p c t", t=TOK))
                b.w("gpsimd", "sg", n_cc1in)
                b.oc("gpsimd", lambda e: e.collective_compute(
                    "AllGather", ALU.bypass, replica_groups=RG,
                    ins=[cc1_in[:]], outs=[cc1_out[:]]), "c1")
                if stage == "a":
                    b.dma("gpsimd", dbg["d_ln1T"][:].rearrange("(c p) t -> p c t", p=P),
                          ln1T_sb[:].rearrange("p (c t) -> p c t", t=TOK))
                    b.w("gpsimd", "sg", b.ctr["sg"])


            with contextlib.ExitStack() as stkB:
                sbB = lambda name, shape, dt: stkB.enter_context(nc.sbuf_tensor(name, shape, dt))
                psB = lambda name, shape, dt: stkB.enter_context(nc.psum_tensor(name, shape, dt))
                rhs_sb = [sbB(f"rhs{i}", [P, HC * 512], BF16) for i in range(2)]
                qT_sb = sbB("qT", [P, SQ], BF16)
                kT_sb = sbB("kT", [P, SQ], BF16)
                v_sb = sbB("v_sb", [P, SQ], BF16)
                ctxT_sb = sbB("ctxT", [P, GL * SQ], BF16)
                vstage = [sbB(f"vstage{i}", [P, 512], BF16) for i in range(2)]
                pT_sb = [sbB(f"pT{i}", [P, 1024], BF16) for i in range(2)]
                tmp_bf = sbB("tmp_bf", [P, 512], BF16)
                Tacc = [sbB(f"Tacc{i}", [P, 512], F32) for i in range(2)]
                invden = sbB("invden", [1, 512], F32)
                bcast_sb = [sbB(f"bcast{i}", [P, 512], F32) for i in range(2)]
                sq_bf = sbB("sq_bf", [P, H], BF16)
                negb_sb = sbB("negb", [P, 1], F32)
                m11 = sbB("m11", [1, 1], F32)
                mx2 = [sbB(f"mx2_{i}", [1, 1], F32) for i in range(2)]
                bq11 = sbB("bq11", [1, 1], F32)
                psum_s = [psB(f"psS{i}", [P, 1024], F32) for i in range(2)]
                psum_ctx = [psB(f"psctx{i}", [P, 512], F32) for i in range(2)]
                psum_bc = psB("psbc", [P, 512], F32)
                psum_trB = psB("pstrB", [P, 512], BF16)

                n_rhs = {}
                n_qkvmm = {}
                n_evac = {}
                n_vtr = [0] * (GL * NQB + 2)
                n_vevac = [0] * (GL * NQB + 2)
                sfree = {0: None, 1: None}
                pT_pv = {0: 0, 1: 0}
                pT_T = {0: 0, 1: 0}
                den_of = [0, 0]
                ctxnorm_of = [0, 0]
                n_bclast = 0
                n_recip = 0
                n_bc = 0
                n_ctxnorm = 0
                n_nrm_busy = 0
                pending_tail = [None]
                n_bclast_cell = [0]
                n_recip_cell = [0]
                n_bc_cell = [0]
                n_ctxnorm_cell = [0]

                for g in range(GL if stage != "a" else 0):
                    for qb in range(NQB):
                        qi = g * NQB + qb
                        b.w("sync", "c1", 1)
                        if qi >= 2:
                            b.w("sync", "t", n_qkvmm.get((qi - 2, 2), 0))
                        r, off = qb // 2, (qb % 2) * 512
                        n_rhs[qi] = b.dma(
                            "sync",
                            rhs_sb[qi % 2][:].rearrange("p (c t) -> p c t", t=512),
                            cc1_out[r * H:(r + 1) * H, off:off + 512].rearrange("(c p) t -> p c t", p=P))
                        b.w("tensor", "sx", n_rhs[qi])
                        b.w("tensor", "sg", n_wq)
                        for dcol in range(3):
                            pb = (qi * 3 + dcol) % 2
                            b.wt("tensor", sfree[pb])
                            col0 = (g * 3 + dcol) * P
                            for hc in range(HC):
                                fn = (lambda e, h=hc, c0=col0, pb=pb, buf=rhs_sb[qi % 2],
                                      first=(hc == 0), last=(hc == HC - 1):
                                      e.matmul(out=psum_s[pb][:, 0:512],
                                               lhsT=wqkv_sb[:, h * 768 + c0: h * 768 + c0 + P],
                                               rhs=buf[:, h * 512:(h + 1) * 512],
                                               start=first, stop=last))
                                if hc < HC - 1:
                                    b.o("tensor", fn)
                                else:
                                    n_mm = b.oi("tensor", fn, "t")
                            n_qkvmm[(qi, dcol)] = n_mm
                            b.w("scalar", "t", n_mm)
                            if dcol == 0:
                                n_ev = b.oi("scalar", lambda e, pb=pb, qb=qb:
                                            e.activation(out=qT_sb[:, qb * 512:(qb + 1) * 512],
                                                         in_=psum_s[pb][:, 0:512], func=AF.Identity,
                                                         bias=0.0, scale=INV_SQRT_HN), "s")
                            elif dcol == 1:
                                n_ev = b.oi("scalar", lambda e, pb=pb, qb=qb:
                                            e.copy(out=kT_sb[:, qb * 512:(qb + 1) * 512],
                                                   in_=psum_s[pb][:, 0:512]), "s")
                            else:
                                if qi >= 2:
                                    b.w("scalar", "t", n_vtr[qi - 2])
                                n_ev = b.oi("scalar", lambda e, pb=pb, vb=vstage[qi % 2]:
                                            e.copy(out=vb[:], in_=psum_s[pb][:, 0:512]), "s")
                                b.w("tensor", "s", n_ev)
                                if qi >= 1:
                                    b.w("tensor", "v", n_vevac[qi - 1])
                                for i in range(4):
                                    n_tr = b.oi("tensor", lambda e, i=i, vb=vstage[qi % 2]:
                                                e.transpose(out=psum_trB[:, i * P:(i + 1) * P],
                                                            in_=vb[:, i * P:(i + 1) * P],
                                                            identity=ident_bf[:]), "t")
                                n_vtr[qi] = n_tr
                                b.w("vector", "t", n_tr)
                                n_vevac[qi] = b.oi("vector", lambda e, qb=qb:
                                                   e.tensor_copy(out=v_sb[:, qb * 512:(qb + 1) * 512],
                                                                 in_=psum_trB[:]), "v")
                            n_evac[(qi, dcol)] = n_ev
                            sfree[pb] = ("s", n_ev)

                    if stage == "qkv" and g == 0:
                        b.w("gpsimd", "v", n_vevac[NQB - 1])
                        b.w("gpsimd", "s", n_evac[(NQB - 1, 1)])
                        for nm, src_ in (("d_qT", qT_sb), ("d_kT", kT_sb), ("d_v", v_sb)):
                            b.dma("gpsimd", dbg[nm][:], src_[:])
                        b.dma("gpsimd", dbg["d_wq"][:], wqkv_sb[:])
                        b.w("gpsimd", "sg", b.ctr["sg"])
                        break

                    b.w("vector", "s", n_evac[(g * NQB + NQB - 1, 1)])
                    for which, src_ in ((0, qT_sb), (1, kT_sb)):
                        first_acc = True
                        for blk in range(4):
                            n_sq = b.oi("vector", lambda e, s_=src_, blk=blk:
                                        e.tensor_mul(out=sq_bf[:], in0=s_[:, blk * H:(blk + 1) * H],
                                                     in1=s_[:, blk * H:(blk + 1) * H]), "v")
                            for j in range(4):
                                b.w("tensor", "v", max(n_sq, n_nrm_busy))
                                b.w("tensor", "s", max(n_bclast, n_bclast_cell[0]))
                                b.w("tensor", "v", max(n_recip, n_recip_cell[0]))
                                n_nmm = b.oi("tensor", lambda e, j=j:
                                             e.matmul(out=psum_bc[0:1, :], lhsT=ones_col_bf[:],
                                                      rhs=sq_bf[:, j * 512:(j + 1) * 512],
                                                      start=True, stop=True), "t")
                                b.w("vector", "t", n_nmm)
                                n_rm11 = b.oi("vector", lambda e: e.reduce_max(out=m11[:], in_=psum_bc[0:1, :],
                                                                               axis=AX.X), "v")
                                b.w("vector", "v", n_rm11)
                                if first_acc:
                                    n_nrm_busy = b.oi("vector", lambda e, w=which:
                                                      e.tensor_copy(out=mx2[w][:], in_=m11[:]), "v")
                                    first_acc = False
                                else:
                                    n_nrm_busy = b.oi("vector", lambda e, w=which:
                                                      e.tensor_tensor(out=mx2[w][:], in0=mx2[w][:],
                                                                      in1=m11[:], op=ALU.max), "v")
                    n_m2 = b.oi("vector", lambda e: e.tensor_mul(out=m11[:], in0=mx2[0][:],
                                                                 in1=mx2[1][:]), "v")
                    b.w("scalar", "v", n_m2)
                    n_bq = b.oi("scalar", lambda e: e.activation(out=bq11[:], in_=m11[:], func=AF.Sqrt,
                                                                 bias=0.0, scale=1.0), "s")
                    b.w("vector", "s", n_bq)
                    n_negb1 = b.oi("vector", lambda e: e.tensor_single_scalar(out=bq11[:], in_=bq11[:],
                                                                              scalar=-1.0, op=ALU.mult), "v")
                    b.w("vector", "v", n_negb1)
                    n_negb1 = b.oi("vector", lambda e: e.tensor_copy(out=m11[:], in_=bq11[:]), "v")
                    b.w("tensor", "v", max(n_negb1, n_ones))
                    b.w("tensor", "s", max(n_bclast, n_bclast_cell[0]))
                    n_nbmm = b.oi("tensor", lambda e: e.matmul(out=psum_bc[:, 0:1],
                                                               lhsT=ones_row_f[:], rhs=bq11[:],
                                                               start=True, stop=True), "t")
                    b.w("scalar", "t", n_nbmm)
                    n_negb = b.oi("scalar", lambda e: e.copy(out=negb_sb[:], in_=psum_bc[:, 0:1]), "s")
                    n_bclast = n_negb
                    n_bclast_cell[0] = n_negb

                    for qb in range(NQB):
                        b.w("tensor", "s", n_negb)
                        n_T = 0
                        n_pv_last = 0
                        NP = NKC // 2

                        def emit_S_exp(p, qb=qb):
                            b_ = p % 2
                            b.wt("tensor", sfree[b_])
                            for half in range(2):
                                kc = 2 * p + half
                                fnS = (lambda e, kc=kc, qb=qb, b_=b_, half=half:
                                       e.matmul(out=psum_s[b_][:, half * 512:(half + 1) * 512],
                                                lhsT=kT_sb[:, kc * P:(kc + 1) * P],
                                                rhs=qT_sb[:, qb * 512:(qb + 1) * 512],
                                                start=True, stop=True))
                                if half == 0:
                                    b.o("tensor", fnS)
                                else:
                                    nS = b.oi("tensor", fnS, "t")
                            b.w("scalar", "t", nS)
                            b.w("scalar", "t", pT_pv[b_])
                            b.w("scalar", "v", pT_T[b_])
                            n_exp = b.oi("scalar", lambda e, b_=b_:
                                         e.activation(out=pT_sb[b_][:], in_=psum_s[b_][:], func=AF.Exp,
                                                      bias=negb_sb[:], scale=1.0), "s")
                            sfree[b_] = ("s", n_exp)
                            return n_exp

                        def emit_PV_T(p, n_exp, qb=qb):
                            nonlocal n_T, n_pv_last
                            b_ = p % 2
                            b.w("tensor", "s", n_exp)
                            if p == 0:
                                b.w("tensor", "v", ctxnorm_of[qb % 2])
                            for half in range(2):
                                kc = 2 * p + half
                                fnP = (lambda e, kc=kc, b_=b_, half=half, qb=qb,
                                       first=(kc == 0), last=(kc == NKC - 1):
                                       e.matmul(out=psum_ctx[qb % 2][:],
                                                lhsT=v_sb[:, kc * P:(kc + 1) * P],
                                                rhs=pT_sb[b_][:, half * 512:(half + 1) * 512],
                                                start=first, stop=last))
                                if half == 0:
                                    b.o("tensor", fnP)
                                else:
                                    n_pv = b.oi("tensor", fnP, "t")
                            pT_pv[b_] = n_pv
                            n_pv_last = n_pv
                            b.w("vector", "s", n_exp)
                            n_tmp = b.oi("vector", lambda e, b_=b_: e.tensor_tensor(
                                out=tmp_bf[:], in0=pT_sb[b_][:, 0:512], in1=pT_sb[b_][:, 512:1024],
                                op=ALU.add), "v")
                            pT_T[b_] = n_tmp
                            if p == 0:
                                b.w("vector", "t", den_of[qb % 2])
                                n_T = b.oi("vector", lambda e, qb=qb:
                                           e.tensor_copy(out=Tacc[qb % 2][:], in_=tmp_bf[:]), "v")
                            else:
                                n_T = b.oi("vector", lambda e, qb=qb:
                                           e.tensor_tensor(out=Tacc[qb % 2][:], in0=Tacc[qb % 2][:],
                                                           in1=tmp_bf[:], op=ALU.add), "v")

                        prev = None
                        for p in range(NP):
                            if p == 4 and pending_tail[0] is not None:
                                pending_tail[0]()
                                pending_tail[0] = None
                            n_exp = emit_S_exp(p)
                            if prev is not None:
                                emit_PV_T(*prev)
                            prev = (p, n_exp)
                        emit_PV_T(*prev)

                        def make_tail(g=g, qb=qb, n_T=n_T, n_pv_last=n_pv_last):
                            def tail():
                                b.w("tensor", "v", n_T)
                                b.w("tensor", "s", n_bclast_cell[0])
                                n_den = b.oi("tensor", lambda e, qb=qb:
                                             e.matmul(out=psum_bc[0:1, :], lhsT=ones_col_f[:],
                                                      rhs=Tacc[qb % 2][:], start=True, stop=True), "t")
                                den_of[qb % 2] = n_den
                                b.w("vector", "t", max(n_den, n_bc_cell[0]))
                                n_rec = b.oi("vector", lambda e: e.reciprocal(out=invden[:],
                                                                              in_=psum_bc[0:1, :]), "v")
                                n_recip_cell[0] = n_rec
                                b.w("tensor", "v", n_rec)
                                n_bc_ = b.oi("tensor", lambda e: e.matmul(out=psum_bc[:],
                                                                          lhsT=ones_row_f[:], rhs=invden[:],
                                                                          start=True, stop=True), "t")
                                n_bc_cell[0] = n_bc_
                                b.w("scalar", "t", n_bc_)
                                n_bcv = b.oi("scalar", lambda e, qb=qb:
                                             e.copy(out=bcast_sb[qb % 2][:], in_=psum_bc[:]), "s")
                                n_bclast_cell[0] = n_bcv
                                b.w("vector", "s", n_bcv)
                                b.w("vector", "t", n_pv_last)
                                n_cn = b.oi("vector", lambda e, g=g, qb=qb: e.tensor_tensor(
                                    out=ctxT_sb[:, g * SQ + qb * 512: g * SQ + (qb + 1) * 512],
                                    in0=psum_ctx[qb % 2][:], in1=bcast_sb[qb % 2][:], op=ALU.mult), "v")
                                ctxnorm_of[qb % 2] = n_cn
                                n_ctxnorm_cell[0] = n_cn
                            return tail
                        pending_tail[0] = make_tail()
                    if pending_tail[0] is not None:
                        pending_tail[0]()
                        pending_tail[0] = None
                    n_bclast = n_bclast_cell[0]
                    n_recip = n_recip_cell[0]
                    n_bc = n_bc_cell[0]
                    n_ctxnorm = n_ctxnorm_cell[0]

                n_B_done = n_qkvmm.get((GL * NQB - 1, 2), 0)
                n_cc2in = 0
                if stage not in ("qkv", "a"):
                    b.w("gpsimd", "v", n_ctxnorm)
                    for g in range(GL):
                        n_cc2in = b.dma("gpsimd", cc2_in[g * P:(g + 1) * P, :],
                                        ctxT_sb[:, g * SQ:(g + 1) * SQ])
                    b.w("gpsimd", "sg", n_cc2in)
                    b.oc("gpsimd", lambda e: e.collective_compute(
                        "AllGather", ALU.bypass, replica_groups=RG,
                        ins=[cc2_in[:]], outs=[cc2_out[:]]), "c2")
                n_C_end = {"t": b.ctr["t"], "s": b.ctr["s"], "v": b.ctr["v"], "sg": b.ctr["sg"]}

        if stage in ("qkv", "a"):
            pass
        elif stage == "ctx":
            tmp = sb("dbgctx", [P, GL * SQ], BF16)
            b.w("sync", "c2", 1)
            for g in range(GL):
                b.dma("sync", tmp[:, g * SQ:(g + 1) * SQ], cc2_in[g * P:(g + 1) * P, :])
            b.w("gpsimd", "sx", b.ctr["sx"])
            for g in range(GL):
                b.dma("gpsimd", dbg["d_ctx"][g * P:(g + 1) * P, :], tmp[:, g * SQ:(g + 1) * SQ])
            b.w("gpsimd", "sg", b.ctr["sg"])
        else:
            with contextlib.ExitStack() as stkD:
                sbD = lambda name, shape, dt: stkD.enter_context(nc.sbuf_tensor(name, shape, dt))
                psD = lambda name, shape, dt: stkD.enter_context(nc.psum_tensor(name, shape, dt))
                x_sb = sbD("x_sb", [P, NT * H], F32)
                psum_d = [psD(f"psd{i}", [P, 512], F32) for i in range(2)]
                psum_bc_d = psD("psbcd", [P, 512], F32)
                psum_trD = psD("pstrD", [P, 512], BF16)

                psd_free = {0: None, 1: None}
                d1scope = contextlib.ExitStack()
                with d1scope:
                    sb1 = lambda name, shape, dt: d1scope.enter_context(nc.sbuf_tensor(name, shape, dt))
                    ctx_sb = sb1("ctx_sb", [P, HC * TOK], BF16)
                    wproj_sb = [sb1(f"wproj{i}", [P, HC * 512], BF16) for i in range(2)]

                    b.w("sync", "t", n_C_end["t"])
                    n_xl = b.dma("sync", x_sb[:].rearrange("p (c d) -> p c d", d=H),
                                 xs[:].rearrange("(c p) d -> p c d", p=P))
                    b.w("sync", "c2", 1)
                    n_ctxl = 0
                    for cc in range(HC):
                        n_ctxl = b.oi("sync", lambda e, cc=cc: e.dma_start(
                            out=ctx_sb[:, cc * TOK:(cc + 1) * TOK],
                            in_=cc2_out[cc * P:(cc + 1) * P, bass.ts(pid_cell["v"], TOK)]), "sx", 16)

                    b.w("tensor", "s", n_C_end["s"])
                    b.w("tensor", "v", n_C_end["v"])
                    n_bse = 0
                    for o4 in range(4):
                        b.w("tensor", "s", n_bse)
                        n_bmm = b.oi("tensor", lambda e, o4=o4: e.matmul(
                            out=psum_bc_d[:], lhsT=ones_row_f[:],
                            rhs=bprojrow[0:1, o4 * 512:(o4 + 1) * 512], start=True, stop=True), "t")
                        b.w("scalar", "t", n_bmm)
                        n_bse = b.oi("scalar", lambda e, o4=o4: e.copy(
                            out=b_sb[:, o4 * 512:(o4 + 1) * 512], in_=psum_bc_d[:]), "s")
                    n_xb = [0] * NT
                    for tt in range(NT):
                        b.w("vector", "sx", n_xl)
                        b.w("vector", "s", n_bse)
                        n_xb[tt] = b.oi("vector", lambda e, tt=tt: e.tensor_tensor(
                            out=x_sb[:, tt * H:(tt + 1) * H], in0=x_sb[:, tt * H:(tt + 1) * H],
                            in1=b_sb[:], op=ALU.add), "v")

                    b.w("gpsimd", "sg", n_cc2in)
                    n_d1 = {}
                    for ob in range(4):
                        if ob >= 2:
                            b.w("gpsimd", "t", n_d1.get((ob - 2, NT - 1), 0))
                        n_wl = b.dma("gpsimd", wproj_sb[ob % 2][:].rearrange("p (c o) -> p c o", o=512),
                                     wprojT[:, ob * 512:(ob + 1) * 512].rearrange("(c p) o -> p c o", p=P))
                        b.w("tensor", "sx", n_ctxl)
                        b.w("tensor", "sg", n_wl)
                        for tt in range(NT):
                            pb = (ob * NT + tt) % 2
                            b.wt("tensor", psd_free[pb])
                            for cc in range(HC):
                                fn = (lambda e, cc=cc, tt=tt, pb=pb, wb=wproj_sb[ob % 2],
                                      first=(cc == 0), last=(cc == HC - 1):
                                      e.matmul(out=psum_d[pb][:],
                                               lhsT=ctx_sb[:, cc * TOK + tt * P: cc * TOK + (tt + 1) * P],
                                               rhs=wb[:, cc * 512:(cc + 1) * 512],
                                               start=first, stop=last))
                                if cc < HC - 1:
                                    b.o("tensor", fn)
                                else:
                                    n_mm = b.oi("tensor", fn, "t")
                            n_d1[(ob, tt)] = n_mm
                            b.w("vector", "t", n_mm)
                            b.w("vector", "v", n_xb[tt])
                            n_ev = b.oi("vector", lambda e, tt=tt, ob=ob, pb=pb: e.tensor_tensor(
                                out=x_sb[:, tt * H + ob * 512: tt * H + (ob + 1) * 512],
                                in0=psum_d[pb][:],
                                in1=x_sb[:, tt * H + ob * 512: tt * H + (ob + 1) * 512],
                                op=ALU.add), "v")
                            psd_free[pb] = ("v", n_ev)

                ln2T_sb = sbD("ln2T", [P, HC * TOK], BF16)
                h1T_sb = sbD("h1T", [P, HC * TOK], BF16)
                alphaE_sb = sbD("alphaE", [P, 4 * TOK], BF16)
                avxT_sb = sbD("avxT", [P, 4 * TOK], BF16)
                uflat_sb = sbD("uflat_b", [P, 4 * H], BF16)

                n_ln2 = [0] * NT
                n_tr2last = [0] * NT
                n_tr2evac = 0
                with contextlib.ExitStack() as stk2:
                    sb2 = lambda name, shape, dt: stk2.enter_context(nc.sbuf_tensor(name, shape, dt))
                    ln2_bf = [sb2(f"ln2bf{i}", [P, H], BF16) for i in range(2)]
                    sq_scrD = sb2("sq_scrD", [P, H], F32)
                    ssumD = sb2("ssumD", [P, 1], F32)
                    negmeanD = [sb2(f"negmeanD{i}", [P, 1], F32) for i in range(2)]
                    varsumD = [sb2(f"varsumD{i}", [P, 1], F32) for i in range(2)]
                    stdvD = [sb2(f"stdvD{i}", [P, 1], F32) for i in range(2)]
                    rstdD = [sb2(f"rstdD{i}", [P, 1], F32) for i in range(2)]
                    nmrD = [sb2(f"nmrD{i}", [P, 1], F32) for i in range(2)]
                    for tt in range(NT):
                        if tt >= 2:
                            b.w("vector", "s", n_ln2[tt - 2])
                        n_redD = b.oi("vector", lambda e, tt=tt: e.reduce_sum(
                            out=ssumD[:], in_=x_sb[:, tt * H:(tt + 1) * H], axis=AX.X), "v")
                        b.w("vector", "v", n_redD)
                        n_nm = b.oi("vector", lambda e, tt=tt: e.tensor_single_scalar(
                            out=negmeanD[tt % 2][:], in_=ssumD[:], scalar=-1.0 / H, op=ALU.mult), "v")
                        b.w("scalar", "v", n_nm)
                        if tt >= 2:
                            b.w("scalar", "t", n_tr2last[tt - 2])
                        n_sqaD = b.oi("scalar", lambda e, tt=tt: e.activation(
                            out=sq_scrD[:], in_=x_sb[:, tt * H:(tt + 1) * H], func=AF.Square,
                            bias=negmeanD[tt % 2][:], scale=1.0, accum_out=varsumD[tt % 2][:]), "s")
                        b.w("scalar", "s", n_sqaD)
                        n_std = b.oi("scalar", lambda e, tt=tt: e.activation(
                            out=stdvD[tt % 2][:], in_=varsumD[tt % 2][:], func=AF.Sqrt,
                            bias=eps_sb[:], scale=1.0 / H), "s")
                        b.w("vector", "s", n_std)
                        n_recD = b.oi("vector", lambda e, tt=tt: e.reciprocal(out=rstdD[tt % 2][:],
                                                                              in_=stdvD[tt % 2][:]), "v")
                        b.w("vector", "v", n_recD)
                        n_nmr = b.oi("vector", lambda e, tt=tt: e.tensor_mul(
                            out=nmrD[tt % 2][:], in0=negmeanD[tt % 2][:], in1=rstdD[tt % 2][:]), "v")
                        b.w("scalar", "v", n_nmr)
                        n_ln2[tt] = b.oi("scalar", lambda e, tt=tt: e.activation(
                            out=ln2_bf[tt % 2][:], in_=x_sb[:, tt * H:(tt + 1) * H], func=AF.Identity,
                            bias=nmrD[tt % 2][:], scale=rstdD[tt % 2][:]), "s")
                        b.w("tensor", "s", n_ln2[tt])
                        for grp in range(4):
                            b.w("tensor", "v", n_tr2evac)
                            for i in range(4):
                                hc = grp * 4 + i
                                n_tr2last[tt] = b.oi("tensor", lambda e, tt=tt, hc=hc, i=i:
                                                     e.transpose(out=psum_trD[:, i * P:(i + 1) * P],
                                                                 in_=ln2_bf[tt % 2][:, hc * P:(hc + 1) * P],
                                                                 identity=ident_bf[:]), "t")
                            b.w("vector", "t", n_tr2last[tt])
                            n_tr2evac = b.oi("vector", lambda e, tt=tt, grp=grp: e.tensor_copy(
                                out=ln2T_sb[:].rearrange("p (c t) -> p c t", t=TOK)
                                    [:, grp * 4:(grp + 1) * 4, tt * P:(tt + 1) * P],
                                in_=psum_trD[:].rearrange("p (i t) -> p i t", t=P)), "v")

                n_h1 = {}
                with contextlib.ExitStack() as stk3:
                    sb3 = lambda name, shape, dt: stk3.enter_context(nc.sbuf_tensor(name, shape, dt))
                    w1_sb = [sb3(f"w1sb{i}", [P, H], BF16) for i in range(2)]
                    b.w("gpsimd", "t", n_tr2last[NT - 1])
                    for ec in range(HC):
                        if ec >= 2:
                            b.w("gpsimd", "t", n_h1.get((ec - 2, 1), 0))
                        n_wl = b.dma("gpsimd", w1_sb[ec % 2][:].rearrange("p (c e) -> p c e", e=P),
                                     w1[:, ec * P:(ec + 1) * P].rearrange("(c p) e -> p c e", p=P))
                        b.w("tensor", "sg", n_wl)
                        b.w("tensor", "v", n_tr2evac)
                        for tb in range(2):
                            pb = (ec * 2 + tb) % 2
                            b.wt("tensor", psd_free[pb])
                            for dc in range(HC):
                                fn = (lambda e, dc=dc, tb=tb, pb=pb, ec=ec,
                                      first=(dc == 0), last=(dc == HC - 1):
                                      e.matmul(out=psum_d[pb][:],
                                               lhsT=w1_sb[ec % 2][:, dc * P:(dc + 1) * P],
                                               rhs=ln2T_sb[:, dc * TOK + tb * 512:
                                                           dc * TOK + (tb + 1) * 512],
                                               start=first, stop=last))
                                if dc < HC - 1:
                                    b.o("tensor", fn)
                                else:
                                    n_mm = b.oi("tensor", fn, "t")
                            n_h1[(ec, tb)] = n_mm
                            b.w("scalar", "t", n_mm)
                            n_ev = b.oi("scalar", lambda e, ec=ec, tb=tb, pb=pb: e.activation(
                                out=h1T_sb[:, ec * TOK + tb * 512: ec * TOK + (tb + 1) * 512],
                                in_=psum_d[pb][:], func=AF.Relu, bias=b1c_sb[:, ec:ec + 1],
                                scale=1.0), "s")
                            psd_free[pb] = ("s", n_ev)

                n_atr = 0
                n_aevac = n_tr2evac
                with contextlib.ExitStack() as stk4:
                    sb4 = lambda name, shape, dt: stk4.enter_context(nc.sbuf_tensor(name, shape, dt))
                    l_sb = sb4("l_sb", [P, 8], F32)
                    negmaxA = sb4("negmaxA", [P, 1], F32)
                    e8 = sb4("e8", [P, 8], F32)
                    denA = sb4("denA", [P, 1], F32)
                    invA = sb4("invA", [P, 1], F32)
                    alpha_bf = sb4("alpha_bf", [P, 8], BF16)
                    b.w("tensor", "sg", n_w2l)
                    b.w("tensor", "s", n_bse)
                    n_b2mm = b.oi("tensor", lambda e: e.matmul(out=psum_bc_d[:, 0:8], lhsT=ones_row_f[:],
                                                               rhs=b2row[:], start=True, stop=True), "t")
                    b.w("scalar", "t", n_b2mm)
                    n_b2e = b.oi("scalar", lambda e: e.copy(out=b2b_sb[:], in_=psum_bc_d[:, 0:8]), "s")
                    n_e8 = 0
                    n_ab = 0
                    n_recA = 0
                    n_atr_d4 = n_tr2last[NT - 1]
                    for tt in range(NT):
                        pb = tt % 2
                        b.wt("tensor", psd_free[pb])
                        for ec in range(HC):
                            fn = (lambda e, ec=ec, tt=tt, pb=pb,
                                  first=(ec == 0), last=(ec == HC - 1):
                                  e.matmul(out=psum_d[pb][:, 0:8],
                                           lhsT=h1T_sb[:, ec * TOK + tt * P: ec * TOK + (tt + 1) * P],
                                           rhs=w2_sb[:, ec * 8:(ec + 1) * 8],
                                           start=first, stop=last))
                            if ec < HC - 1:
                                b.o("tensor", fn)
                            else:
                                n_mm = b.oi("tensor", fn, "t")
                        b.w("vector", "t", n_mm)
                        b.w("vector", "s", max(n_b2e, n_e8))
                        n_l = b.oi("vector", lambda e, pb=pb: e.tensor_tensor(
                            out=l_sb[:], in0=psum_d[pb][:, 0:8], in1=b2b_sb[:], op=ALU.add), "v")
                        psd_free[pb] = ("v", n_l)
                        n_rm = b.oi("vector", lambda e: e.reduce_max(out=negmaxA[:], in_=l_sb[:],
                                                                     axis=AX.X, negate=True), "v")
                        b.w("scalar", "v", max(n_rm, n_recA, n_ab))
                        n_e8 = b.oi("scalar", lambda e: e.activation(out=e8[:], in_=l_sb[:], func=AF.Exp,
                                                                     bias=negmaxA[:], scale=1.0,
                                                                     accum_out=denA[:]), "s")
                        b.w("vector", "s", n_e8)
                        n_recA = b.oi("vector", lambda e: e.reciprocal(out=invA[:], in_=denA[:]), "v")
                        b.w("vector", "v", n_recA)
                        b.w("vector", "t", n_atr)
                        n_ab = b.oi("vector", lambda e: e.tensor_scalar_mul(out=alpha_bf[:], in0=e8[:],
                                                                            scalar1=invA[:]), "v")
                        b.w("tensor", "v", max(n_ab, n_aevac))
                        b.w("tensor", "t", n_atr_d4)
                        n_atr = b.oi("tensor", lambda e: e.transpose(out=psum_trD[0:8, 0:P],
                                                                     in_=alpha_bf[:],
                                                                     identity=ident_bf[:]), "t")
                        n_atr_d4 = n_atr
                        b.w("vector", "t", n_atr)
                        n_aevac = b.oi("vector", lambda e, tt=tt: e.tensor_copy(
                            out=alphaT_sb[:, tt * P:(tt + 1) * P], in_=psum_trD[0:8, 0:P]), "v")

                n_aE = n_b2e
                for krc in range(4):
                    for tb in range(2):
                        b.w("tensor", "v", n_aevac)
                        b.w("tensor", "s", n_aE)
                        n_mm = b.oi("tensor", lambda e, krc=krc, tb=tb: e.matmul(
                            out=psum_bc_d[:], lhsT=sel_sb[:, krc * P:(krc + 1) * P],
                            rhs=alphaT_sb[:, tb * 512:(tb + 1) * 512], start=True, stop=True), "t")
                        b.w("scalar", "t", n_mm)
                        n_aE = b.oi("scalar", lambda e, krc=krc, tb=tb: e.copy(
                            out=alphaE_sb[:, krc * TOK + tb * 512: krc * TOK + (tb + 1) * 512],
                            in_=psum_bc_d[:]), "s")

                n_avx = 0
                with contextlib.ExitStack() as stk6:
                    sb6 = lambda name, shape, dt: stk6.enter_context(nc.sbuf_tensor(name, shape, dt))
                    vflat_sb = sb6("vflat", [P, HC * 512], BF16)
                    b.w("gpsimd", "t", n_atr)
                    b.w("gpsimd", "s", n_aE)
                    n_vfl = b.dma("gpsimd", vflat_sb[:].rearrange("p (c k) -> p c k", k=512),
                                  vflatT[:].rearrange("(c p) k -> p c k", p=P))
                    n_ufl = b.dma("gpsimd", uflat_sb[:].rearrange("p (c d) -> p c d", d=H),
                                  uflat[:].rearrange("(c p) d -> p c d", p=P))

                    for krc in range(4):
                        for tb in range(2):
                            pb = (krc * 2 + tb) % 2
                            b.w("tensor", "sg", n_vfl)
                            b.wt("tensor", psd_free[pb])
                            for dc in range(HC):
                                fn = (lambda e, dc=dc, krc=krc, tb=tb, pb=pb,
                                      first=(dc == 0), last=(dc == HC - 1):
                                      e.matmul(out=psum_d[pb][:],
                                               lhsT=vflat_sb[:, dc * 512 + krc * P:
                                                             dc * 512 + (krc + 1) * P],
                                               rhs=ln2T_sb[:, dc * TOK + tb * 512:
                                                           dc * TOK + (tb + 1) * 512],
                                               start=first, stop=last))
                                if dc < HC - 1:
                                    b.o("tensor", fn)
                                else:
                                    n_mm = b.oi("tensor", fn, "t")
                            b.w("vector", "t", n_mm)
                            b.w("vector", "s", n_aE)
                            n_avx = b.oi("vector", lambda e, krc=krc, tb=tb, pb=pb: e.tensor_tensor(
                                out=avxT_sb[:, krc * TOK + tb * 512: krc * TOK + (tb + 1) * 512],
                                in0=psum_d[pb][:],
                                in1=alphaE_sb[:, krc * TOK + tb * 512: krc * TOK + (tb + 1) * 512],
                                op=ALU.mult), "v")
                            psd_free[pb] = ("v", n_avx)

                with contextlib.ExitStack() as stk7:
                    sb7 = lambda name, shape, dt: stk7.enter_context(nc.sbuf_tensor(name, shape, dt))
                    out_sb = [sb7(f"outsb{i}", [P, H], F32) for i in range(2)]
                    n_store = [0] * NT
                    for tt in range(NT):
                        n_ev = 0
                        for ob in range(4):
                            pb = (tt * 4 + ob) % 2
                            b.w("tensor", "sg", n_ufl)
                            b.w("tensor", "v", n_avx)
                            b.wt("tensor", psd_free[pb])
                            for krc in range(4):
                                fn = (lambda e, krc=krc, tt=tt, ob=ob, pb=pb,
                                      first=(krc == 0), last=(krc == 3):
                                      e.matmul(out=psum_d[pb][:],
                                               lhsT=avxT_sb[:, krc * TOK + tt * P:
                                                            krc * TOK + (tt + 1) * P],
                                               rhs=uflat_sb[:, krc * H + ob * 512:
                                                            krc * H + (ob + 1) * 512],
                                               start=first, stop=last))
                                if krc < 3:
                                    b.o("tensor", fn)
                                else:
                                    n_mm = b.oi("tensor", fn, "t")
                            b.w("vector", "t", n_mm)
                            if tt >= 2 and ob == 0:
                                b.w("vector", "sg", n_store[tt - 2])
                            n_ev = b.oi("vector", lambda e, tt=tt, ob=ob, pb=pb: e.tensor_tensor(
                                out=out_sb[tt % 2][:, ob * 512:(ob + 1) * 512],
                                in0=psum_d[pb][:],
                                in1=x_sb[:, tt * H + ob * 512: tt * H + (ob + 1) * 512],
                                op=ALU.add), "v")
                            psd_free[pb] = ("v", n_ev)
                        b.w("gpsimd", "v", n_ev)
                        n_store[tt] = b.dma("gpsimd", out[tt * P:(tt + 1) * P, :], out_sb[tt % 2][:])
                    b.w("gpsimd", "sg", n_store[NT - 1])

        with nc.Block() as block:
            @block.sync
            def _(eng):
                for fn in b.prog["sync"]:
                    fn(eng)

            @block.tensor
            def _(eng):
                for fn in b.prog["tensor"]:
                    fn(eng)

            @block.scalar
            def _(eng):
                for fn in b.prog["scalar"]:
                    fn(eng)

            @block.vector
            def _(eng):
                for fn in b.prog["vector"]:
                    fn(eng)

            @block.gpsimd
            def _(eng):
                for fn in b.prog["gpsimd"]:
                    fn(eng)

    return nc


def _host_inputs(inputs):
    f32 = np.float32
    q = np.asarray(inputs["L0_attn_query"], dtype=f32).reshape(SQ, H)
    Wqkv = np.asarray(inputs["L0_attn_Wqkv"], dtype=f32)
    Wproj = np.asarray(inputs["L0_attn_Wproj"], dtype=f32)
    bproj = np.asarray(inputs["L0_attn_bproj"], dtype=f32).reshape(1, H)
    U = np.asarray(inputs["L0_ffn_U"], dtype=f32)
    V = np.asarray(inputs["L0_ffn_V"], dtype=f32)
    W1 = np.asarray(inputs["L0_ffn_W1"], dtype=f32)
    b1 = np.asarray(inputs["L0_ffn_b1"], dtype=f32)
    W2 = np.asarray(inputs["L0_ffn_W2"], dtype=f32)
    b2 = np.asarray(inputs["L0_ffn_b2"], dtype=f32).reshape(1, 8)

    WqkvT = np.ascontiguousarray(Wqkv.T)
    WprojT = np.ascontiguousarray(Wproj.T)
    b1c = np.ascontiguousarray(b1.reshape(HC, P).T)
    VflatT = np.ascontiguousarray(V.reshape(512, H).T)
    Uflat = np.ascontiguousarray(U.transpose(0, 2, 1).reshape(512, H))
    sel = np.zeros((8, 512), dtype=f32)
    for k in range(8):
        sel[k, k * 64:(k + 1) * 64] = 1.0
    identity = np.eye(P, dtype=f32)

    in_maps = []
    for c in range(8):
        in_maps.append({
            "xs": np.ascontiguousarray(q[c * TOK:(c + 1) * TOK, :]),
            "wqkvT": np.ascontiguousarray(WqkvT[:, c * GL * 3 * P:(c + 1) * GL * 3 * P]),
            "wprojT": WprojT,
            "bproj": bproj,
            "w1": W1,
            "b1c": b1c,
            "w2": W2,
            "b2": b2,
            "vflatT": VflatT,
            "uflat": Uflat,
            "sel": sel,
            "ident": identity,
        })
    return in_maps


def kernel(**inputs):
    inputs.pop("L0_attn_key", None)
    inputs.pop("L0_attn_value", None)
    nc = build("full")
    in_maps = _host_inputs(inputs)
    res = run_bass_kernel_spmd(nc, in_maps, core_ids=list(range(8)))
    full = np.concatenate([res.results[c]["out"] for c in range(8)], axis=0)
    return full.reshape(1, SQ, H)


# version 09
# speedup vs baseline: 1.7255x; 1.0776x over previous
import contextlib
import numpy as np
import concourse.bass as bass
import concourse.mybir as mybir
from concourse.bass_utils import run_bass_kernel_spmd

F32 = mybir.dt.float32
BF16 = mybir.dt.bfloat16
AF = mybir.ActivationFunctionType
ALU = mybir.AluOpType
AX = mybir.AxisListType
ET = mybir.EngineType

P = 128
SQ = 8192
H = 2048
TOK = 1024
NT = TOK // P
HC = H // P
NQB = SQ // 512
NKC = SQ // P
GL = 2
RG = [[0, 1, 2, 3, 4, 5, 6, 7]]
INV_SQRT_HN = 0.08838834764831845
EPS = 1e-5

ENGINES = ("sync", "tensor", "scalar", "vector", "gpsimd")


class B:

    def __init__(self, nc, stk):
        self.nc = nc
        self.prog = {e: [] for e in ENGINES}
        self.sem = {}
        self.ctr = {}
        self.waited = {e: {} for e in ENGINES}
        for name in ("sx", "sg", "c1", "c1b", "c2", "c2b", "t", "s", "v"):
            self.sem[name] = stk.enter_context(nc.semaphore(name))
            self.ctr[name] = 0

    def w(self, e, s, val):
        if not val or val <= 0:
            return
        if self.waited[e].get(s, 0) >= val:
            return
        self.waited[e][s] = val
        h = self.sem[s]
        self.prog[e].append(lambda eng, h=h, val=val: eng.wait_ge(h, val))

    def wt(self, e, tag):
        if tag:
            self.w(e, tag[0], tag[1])

    def o(self, e, fn):
        self.prog[e].append(fn)

    def oi(self, e, fn, s, k=1):
        self.ctr[s] += k
        n = self.ctr[s]
        h = self.sem[s]
        self.prog[e].append(lambda eng, fn=fn, h=h, k=k: fn(eng).then_inc(h, k))
        return n

    def oc(self, e, fn, s):
        self.ctr[s] += 1
        h = self.sem[s]
        self.prog[e].append(lambda eng, fn=fn, h=h: fn(eng).then_inc(h))
        return self.ctr[s]

    def dma(self, e, out, in_, s=None):
        if s is None:
            s = "sg" if e == "gpsimd" else "sx"
        return self.oi(e, lambda eng, out=out, in_=in_: eng.dma_start(out=out, in_=in_), s, 16)


def build(stage="full"):
    nc = bass.Bass("TRN2", target_bir_lowering=False, num_devices=8)

    xs = nc.dram_tensor("xs", [TOK, H], F32, kind="ExternalInput").ap()
    wqkvT = nc.dram_tensor("wqkvT", [H, GL * 3 * P], F32, kind="ExternalInput").ap()
    wprojT = nc.dram_tensor("wprojT", [H, H], F32, kind="ExternalInput").ap()
    bproj = nc.dram_tensor("bproj", [1, H], F32, kind="ExternalInput").ap()
    w1 = nc.dram_tensor("w1", [H, H], F32, kind="ExternalInput").ap()
    b1c = nc.dram_tensor("b1c", [P, HC], F32, kind="ExternalInput").ap()
    w2 = nc.dram_tensor("w2", [H, 8], F32, kind="ExternalInput").ap()
    b2 = nc.dram_tensor("b2", [1, 8], F32, kind="ExternalInput").ap()
    vflatT = nc.dram_tensor("vflatT", [H, 512], F32, kind="ExternalInput").ap()
    uflat = nc.dram_tensor("uflat", [512, H], F32, kind="ExternalInput").ap()
    sel = nc.dram_tensor("sel", [8, 512], F32, kind="ExternalInput").ap()
    ident = nc.dram_tensor("ident", [P, P], F32, kind="ExternalInput").ap()
    out = nc.dram_tensor("out", [TOK, H], F32, kind="ExternalOutput").ap()

    cc1_in = nc.dram_tensor("cc1_in", [H, TOK], BF16, kind="Internal").ap()
    cc1_out = nc.dram_tensor("cc1_out", [8 * H, TOK], BF16, kind="Internal").ap()
    cc2_in = [nc.dram_tensor(f"cc2_in{i}", [P, SQ], BF16, kind="Internal").ap()
              for i in range(2)]
    cc2_out = [nc.dram_tensor(f"cc2_out{i}", [8 * P, SQ], BF16, kind="Internal",
                              addr_space="Shared").ap()
               for i in range(2)]

    dbg = {}
    if stage == "qkv":
        for nm in ("d_qT", "d_kT", "d_v"):
            dbg[nm] = nc.dram_tensor(nm, [P, SQ], F32, kind="ExternalOutput").ap()
        dbg["d_wq"] = nc.dram_tensor("d_wq", [P, HC * GL * 3 * P], F32, kind="ExternalOutput").ap()
    if stage == "ctx":
        dbg["d_ctx"] = nc.dram_tensor("d_ctx", [GL * P, SQ], F32, kind="ExternalOutput").ap()
    if stage == "a":
        dbg["d_ln1T"] = nc.dram_tensor("d_ln1T", [H, TOK], F32, kind="ExternalOutput").ap()

    stk = contextlib.ExitStack()
    with stk:
        b = B(nc, stk)
        sb = lambda name, shape, dt: stk.enter_context(nc.sbuf_tensor(name, shape, dt))

        ident_bf = sb("ident_bf", [P, P], BF16)
        ones_col_f = sb("ones_col_f", [P, 1], F32)
        ones_col_bf = sb("ones_col_bf", [P, 1], BF16)
        ones_row_f = sb("ones_row_f", [1, P], F32)
        bprojrow = sb("bprojrow", [1, H], F32)
        b_sb = sb("b_sb", [P, H], F32)
        b1c_sb = sb("b1c_sb", [P, HC], F32)
        w2_sb = sb("w2_sb", [P, HC * 8], BF16)
        b2row = sb("b2row", [1, 8], F32)
        b2b_sb = sb("b2b_sb", [P, 8], F32)
        sel_sb = sb("sel_sb", [8, 512], BF16)
        alphaT_sb = sb("alphaT_sb", [8, TOK], BF16)
        eps_sb = sb("eps_sb", [P, 1], F32)

        n_id = b.dma("gpsimd", ident_bf[:], ident[:])
        b.dma("sync", b1c_sb[:], b1c[:])
        b.dma("sync", bprojrow[:], bproj[:])
        b.dma("sync", b2row[:], b2[:])
        b.dma("gpsimd", sel_sb[:], sel[:])
        n_w2l = b.dma("gpsimd", w2_sb[:].rearrange("p (c k) -> p c k", k=8),
                      w2[:].rearrange("(c p) k -> p c k", p=P))
        b.oi("vector", lambda e: e.memset(ones_col_f[:], 1.0), "v")
        b.oi("vector", lambda e: e.memset(ones_col_bf[:], 1.0), "v")
        n_ones = b.oi("vector", lambda e: e.memset(ones_row_f[:], 1.0), "v")
        b.oi("vector", lambda e: e.memset(eps_sb[:], EPS), "v")

        pid_cell = {}
        def _getpid(eng):
            pid_cell["v"] = nc.partition_id(engines=[ET.SP])
        b.o("sync", _getpid)

        with contextlib.ExitStack() as stkW:
            sbW = lambda name, shape, dt: stkW.enter_context(nc.sbuf_tensor(name, shape, dt))
            wqkv_sb = sbW("wqkv_sb", [P, HC * GL * 3 * P], BF16)
            n_wq = b.dma("gpsimd", wqkv_sb[:].rearrange("p (c f) -> p c f", f=GL * 3 * P),
                         wqkvT[:].rearrange("(c p) f -> p c f", p=P))

            with contextlib.ExitStack() as stkA:
                sbA = lambda name, shape, dt: stkA.enter_context(nc.sbuf_tensor(name, shape, dt))
                psA = lambda name, shape, dt: stkA.enter_context(nc.psum_tensor(name, shape, dt))
                xtile = [sbA(f"xtile{i}", [P, H], F32) for i in range(2)]
                ln1_bf = [sbA(f"ln1bf{i}", [P, H], BF16) for i in range(2)]
                sq_scr = sbA("sq_scr", [P, H], F32)
                ln1T_sb = sbA("ln1T", [P, HC * TOK], BF16)
                ssum = sbA("ssum", [P, 1], F32)
                negmean = [sbA(f"negmean{i}", [P, 1], F32) for i in range(2)]
                varsum = [sbA(f"varsum{i}", [P, 1], F32) for i in range(2)]
                stdv = [sbA(f"stdv{i}", [P, 1], F32) for i in range(2)]
                rstd = [sbA(f"rstd{i}", [P, 1], F32) for i in range(2)]
                nmr = [sbA(f"nmr{i}", [P, 1], F32) for i in range(2)]
                psum_trA = [psA(f"pstrA{i}", [P, P], BF16) for i in range(2)]

                n_xload = [0] * NT
                n_ln1 = [0] * NT
                n_trlast = [0] * NT
                trA_free = {0: 0, 1: 0}
                last_evac = 0
                half_evac = [0, 0]
                for tt in range(NT):
                    if tt >= 2:
                        b.w("sync", "s", n_ln1[tt - 2])
                    n_xload[tt] = b.dma("sync", xtile[tt % 2][:], xs[tt * P:(tt + 1) * P, :])

                    if tt >= 2:
                        b.w("vector", "s", n_ln1[tt - 2])
                    b.w("vector", "sx", n_xload[tt])
                    n_red = b.oi("vector", lambda e, x=xtile[tt % 2]:
                                 e.reduce_sum(out=ssum[:], in_=x[:], axis=AX.X), "v")
                    b.w("vector", "v", n_red)
                    n_nm = b.oi("vector", lambda e, nm=negmean[tt % 2]:
                                e.tensor_single_scalar(out=nm[:], in_=ssum[:], scalar=-1.0 / H,
                                                       op=ALU.mult), "v")

                    b.w("scalar", "v", n_nm)
                    if tt >= 2:
                        b.w("scalar", "t", n_trlast[tt - 2])
                    n_sqa = b.oi("scalar", lambda e, x=xtile[tt % 2], nm=negmean[tt % 2], vs=varsum[tt % 2]:
                                 e.activation(out=sq_scr[:], in_=x[:], func=AF.Square, bias=nm[:], scale=1.0,
                                              accum_out=vs[:]), "s")
                    b.w("scalar", "s", n_sqa)
                    n_std = b.oi("scalar", lambda e, vs=varsum[tt % 2], sd=stdv[tt % 2]:
                                 e.activation(out=sd[:], in_=vs[:], func=AF.Sqrt, bias=eps_sb[:],
                                              scale=1.0 / H), "s")

                    b.w("vector", "s", n_std)
                    n_rec = b.oi("vector", lambda e, sd=stdv[tt % 2], rs=rstd[tt % 2]:
                                 e.reciprocal(out=rs[:], in_=sd[:]), "v")
                    b.w("vector", "v", n_rec)
                    n_nmr = b.oi("vector", lambda e, nm=negmean[tt % 2], rs=rstd[tt % 2], m=nmr[tt % 2]:
                                 e.tensor_mul(out=m[:], in0=nm[:], in1=rs[:]), "v")

                    b.w("scalar", "v", n_nmr)
                    n_ln1[tt] = b.oi("scalar", lambda e, x=xtile[tt % 2], rs=rstd[tt % 2],
                                     m=nmr[tt % 2], o=ln1_bf[tt % 2]:
                                     e.activation(out=o[:], in_=x[:], func=AF.Identity,
                                                  bias=m[:], scale=rs[:]), "s")

                    b.w("tensor", "s", n_ln1[tt])
                    b.w("tensor", "sg", n_id)
                    for hc in range(HC):
                        pb = hc % 2
                        b.w("tensor", "v", trA_free[pb])
                        n_trlast[tt] = b.oi("tensor", lambda e, i=ln1_bf[tt % 2], h=hc, o=psum_trA[pb]:
                                            e.transpose(out=o[:], in_=i[:, h * P:(h + 1) * P],
                                                        identity=ident_bf[:]), "t")
                        b.w("vector", "t", n_trlast[tt])
                        last_evac = b.oi("vector", lambda e, h=hc, t_=tt, i=psum_trA[pb]:
                                         e.tensor_copy(out=ln1T_sb[:, h * TOK + t_ * P:
                                                                   h * TOK + (t_ + 1) * P],
                                                       in_=i[:]), "v")
                        trA_free[pb] = last_evac
                    if tt == 3:
                        half_evac[0] = last_evac
                half_evac[1] = last_evac

                b.w("gpsimd", "v", half_evac[1])
                n_cc1in = b.dma("gpsimd",
                                cc1_in[:].rearrange("(c p) t -> p c t", p=P),
                                ln1T_sb[:].rearrange("p (c t) -> p c t", t=TOK))
                b.w("gpsimd", "sg", n_cc1in)
                b.oc("gpsimd", lambda e: e.collective_compute(
                    "AllGather", ALU.bypass, replica_groups=RG,
                    ins=[cc1_in[:]], outs=[cc1_out[:]]), "c1")
                if stage == "a":
                    b.dma("gpsimd", dbg["d_ln1T"][:].rearrange("(c p) t -> p c t", p=P),
                          ln1T_sb[:].rearrange("p (c t) -> p c t", t=TOK))
                    b.w("gpsimd", "sg", b.ctr["sg"])


            with contextlib.ExitStack() as stkB:
                sbB = lambda name, shape, dt: stkB.enter_context(nc.sbuf_tensor(name, shape, dt))
                psB = lambda name, shape, dt: stkB.enter_context(nc.psum_tensor(name, shape, dt))
                rhs_sb = [sbB(f"rhs{i}", [P, HC * 512], BF16) for i in range(2)]
                qT_sb = sbB("qT", [P, SQ], BF16)
                kT_sb = sbB("kT", [P, SQ], BF16)
                v_sb = sbB("v_sb", [P, SQ], BF16)
                ctxT_sb = sbB("ctxT", [P, GL * SQ], BF16)
                vstage = [sbB(f"vstage{i}", [P, 512], BF16) for i in range(2)]
                pT_sb = [sbB(f"pT{i}", [P, 1024], BF16) for i in range(2)]
                tmp_bf = sbB("tmp_bf", [P, 512], BF16)
                Tacc = [sbB(f"Tacc{i}", [P, 512], F32) for i in range(2)]
                invden = sbB("invden", [1, 512], F32)
                bcast_sb = [sbB(f"bcast{i}", [P, 512], F32) for i in range(2)]
                sq_bf = sbB("sq_bf", [P, H], BF16)
                negb_sb = sbB("negb", [P, 1], F32)
                m11 = sbB("m11", [1, 1], F32)
                mx2 = [sbB(f"mx2_{i}", [1, 1], F32) for i in range(2)]
                bq11 = sbB("bq11", [1, 1], F32)
                psum_s = [psB(f"psS{i}", [P, 1024], F32) for i in range(2)]
                psum_ctx = [psB(f"psctx{i}", [P, 512], F32) for i in range(2)]
                psum_bc = psB("psbc", [P, 512], F32)
                psum_trB = psB("pstrB", [P, 512], BF16)

                n_rhs = {}
                n_qkvmm = {}
                n_evac = {}
                n_vtr = [0] * (GL * NQB + 2)
                n_vevac = [0] * (GL * NQB + 2)
                sfree = {0: None, 1: None}
                pT_pv = {0: 0, 1: 0}
                pT_T = {0: 0, 1: 0}
                den_of = [0, 0]
                ctxnorm_of = [0, 0]
                n_bclast = 0
                n_recip = 0
                n_bc = 0
                n_ctxnorm = 0
                n_nrm_busy = 0
                pending_tail = [None]
                n_cc2in = 0
                n_bclast_cell = [0]
                n_recip_cell = [0]
                n_bc_cell = [0]
                n_ctxnorm_cell = [0]

                for g in range(GL if stage != "a" else 0):
                    for qb in range(NQB):
                        qi = g * NQB + qb
                        b.w("sync", "c1", 1)
                        if qi >= 2:
                            b.w("sync", "t", n_qkvmm.get((qi - 2, 2), 0))
                        r, off = qb // 2, (qb % 2) * 512
                        n_rhs[qi] = b.dma(
                            "sync",
                            rhs_sb[qi % 2][:].rearrange("p (c t) -> p c t", t=512),
                            cc1_out[r * H:(r + 1) * H, off:off + 512].rearrange("(c p) t -> p c t", p=P))
                        b.w("tensor", "sx", n_rhs[qi])
                        b.w("tensor", "sg", n_wq)
                        for dcol in range(3):
                            pb = (qi * 3 + dcol) % 2
                            b.wt("tensor", sfree[pb])
                            col0 = (g * 3 + dcol) * P
                            for hc in range(HC):
                                fn = (lambda e, h=hc, c0=col0, pb=pb, buf=rhs_sb[qi % 2],
                                      first=(hc == 0), last=(hc == HC - 1):
                                      e.matmul(out=psum_s[pb][:, 0:512],
                                               lhsT=wqkv_sb[:, h * 768 + c0: h * 768 + c0 + P],
                                               rhs=buf[:, h * 512:(h + 1) * 512],
                                               start=first, stop=last))
                                if hc < HC - 1:
                                    b.o("tensor", fn)
                                else:
                                    n_mm = b.oi("tensor", fn, "t")
                            n_qkvmm[(qi, dcol)] = n_mm
                            b.w("scalar", "t", n_mm)
                            if dcol == 0:
                                n_ev = b.oi("scalar", lambda e, pb=pb, qb=qb:
                                            e.activation(out=qT_sb[:, qb * 512:(qb + 1) * 512],
                                                         in_=psum_s[pb][:, 0:512], func=AF.Identity,
                                                         bias=0.0, scale=INV_SQRT_HN), "s")
                            elif dcol == 1:
                                n_ev = b.oi("scalar", lambda e, pb=pb, qb=qb:
                                            e.copy(out=kT_sb[:, qb * 512:(qb + 1) * 512],
                                                   in_=psum_s[pb][:, 0:512]), "s")
                            else:
                                if qi >= 2:
                                    b.w("scalar", "t", n_vtr[qi - 2])
                                n_ev = b.oi("scalar", lambda e, pb=pb, vb=vstage[qi % 2]:
                                            e.copy(out=vb[:], in_=psum_s[pb][:, 0:512]), "s")
                                b.w("tensor", "s", n_ev)
                                if qi >= 1:
                                    b.w("tensor", "v", n_vevac[qi - 1])
                                for i in range(4):
                                    n_tr = b.oi("tensor", lambda e, i=i, vb=vstage[qi % 2]:
                                                e.transpose(out=psum_trB[:, i * P:(i + 1) * P],
                                                            in_=vb[:, i * P:(i + 1) * P],
                                                            identity=ident_bf[:]), "t")
                                n_vtr[qi] = n_tr
                                b.w("vector", "t", n_tr)
                                n_vevac[qi] = b.oi("vector", lambda e, qb=qb:
                                                   e.tensor_copy(out=v_sb[:, qb * 512:(qb + 1) * 512],
                                                                 in_=psum_trB[:]), "v")
                            n_evac[(qi, dcol)] = n_ev
                            sfree[pb] = ("s", n_ev)

                    if stage == "qkv" and g == 0:
                        b.w("gpsimd", "v", n_vevac[NQB - 1])
                        b.w("gpsimd", "s", n_evac[(NQB - 1, 1)])
                        for nm, src_ in (("d_qT", qT_sb), ("d_kT", kT_sb), ("d_v", v_sb)):
                            b.dma("gpsimd", dbg[nm][:], src_[:])
                        b.dma("gpsimd", dbg["d_wq"][:], wqkv_sb[:])
                        b.w("gpsimd", "sg", b.ctr["sg"])
                        break

                    b.w("vector", "s", n_evac[(g * NQB + NQB - 1, 1)])
                    for which, src_ in ((0, qT_sb), (1, kT_sb)):
                        first_acc = True
                        for blk in range(4):
                            n_sq = b.oi("vector", lambda e, s_=src_, blk=blk:
                                        e.tensor_mul(out=sq_bf[:], in0=s_[:, blk * H:(blk + 1) * H],
                                                     in1=s_[:, blk * H:(blk + 1) * H]), "v")
                            for j in range(4):
                                b.w("tensor", "v", max(n_sq, n_nrm_busy))
                                b.w("tensor", "s", max(n_bclast, n_bclast_cell[0]))
                                b.w("tensor", "v", max(n_recip, n_recip_cell[0]))
                                n_nmm = b.oi("tensor", lambda e, j=j:
                                             e.matmul(out=psum_bc[0:1, :], lhsT=ones_col_bf[:],
                                                      rhs=sq_bf[:, j * 512:(j + 1) * 512],
                                                      start=True, stop=True), "t")
                                b.w("vector", "t", n_nmm)
                                n_rm11 = b.oi("vector", lambda e: e.reduce_max(out=m11[:], in_=psum_bc[0:1, :],
                                                                               axis=AX.X), "v")
                                b.w("vector", "v", n_rm11)
                                if first_acc:
                                    n_nrm_busy = b.oi("vector", lambda e, w=which:
                                                      e.tensor_copy(out=mx2[w][:], in_=m11[:]), "v")
                                    first_acc = False
                                else:
                                    n_nrm_busy = b.oi("vector", lambda e, w=which:
                                                      e.tensor_tensor(out=mx2[w][:], in0=mx2[w][:],
                                                                      in1=m11[:], op=ALU.max), "v")
                    n_m2 = b.oi("vector", lambda e: e.tensor_mul(out=m11[:], in0=mx2[0][:],
                                                                 in1=mx2[1][:]), "v")
                    b.w("scalar", "v", n_m2)
                    n_bq = b.oi("scalar", lambda e: e.activation(out=bq11[:], in_=m11[:], func=AF.Sqrt,
                                                                 bias=0.0, scale=1.0), "s")
                    b.w("vector", "s", n_bq)
                    n_negb1 = b.oi("vector", lambda e: e.tensor_single_scalar(out=bq11[:], in_=bq11[:],
                                                                              scalar=-1.0, op=ALU.mult), "v")
                    b.w("vector", "v", n_negb1)
                    n_negb1 = b.oi("vector", lambda e: e.tensor_copy(out=m11[:], in_=bq11[:]), "v")
                    b.w("tensor", "v", max(n_negb1, n_ones))
                    b.w("tensor", "s", max(n_bclast, n_bclast_cell[0]))
                    n_nbmm = b.oi("tensor", lambda e: e.matmul(out=psum_bc[:, 0:1],
                                                               lhsT=ones_row_f[:], rhs=bq11[:],
                                                               start=True, stop=True), "t")
                    b.w("scalar", "t", n_nbmm)
                    n_negb = b.oi("scalar", lambda e: e.copy(out=negb_sb[:], in_=psum_bc[:, 0:1]), "s")
                    n_bclast = n_negb
                    n_bclast_cell[0] = n_negb

                    for qb in range(NQB):
                        b.w("tensor", "s", n_negb)
                        n_T = 0
                        n_pv_last = 0
                        NP = NKC // 2

                        def emit_S_exp(p, qb=qb):
                            b_ = p % 2
                            b.wt("tensor", sfree[b_])
                            for half in range(2):
                                kc = 2 * p + half
                                fnS = (lambda e, kc=kc, qb=qb, b_=b_, half=half:
                                       e.matmul(out=psum_s[b_][:, half * 512:(half + 1) * 512],
                                                lhsT=kT_sb[:, kc * P:(kc + 1) * P],
                                                rhs=qT_sb[:, qb * 512:(qb + 1) * 512],
                                                start=True, stop=True))
                                if half == 0:
                                    b.o("tensor", fnS)
                                else:
                                    nS = b.oi("tensor", fnS, "t")
                            b.w("scalar", "t", nS)
                            b.w("scalar", "t", pT_pv[b_])
                            b.w("scalar", "v", pT_T[b_])
                            n_exp = b.oi("scalar", lambda e, b_=b_:
                                         e.activation(out=pT_sb[b_][:], in_=psum_s[b_][:], func=AF.Exp,
                                                      bias=negb_sb[:], scale=1.0), "s")
                            sfree[b_] = ("s", n_exp)
                            return n_exp

                        def emit_PV_T(p, n_exp, qb=qb):
                            nonlocal n_T, n_pv_last
                            b_ = p % 2
                            b.w("tensor", "s", n_exp)
                            if p == 0:
                                b.w("tensor", "v", ctxnorm_of[qb % 2])
                            for half in range(2):
                                kc = 2 * p + half
                                fnP = (lambda e, kc=kc, b_=b_, half=half, qb=qb,
                                       first=(kc == 0), last=(kc == NKC - 1):
                                       e.matmul(out=psum_ctx[qb % 2][:],
                                                lhsT=v_sb[:, kc * P:(kc + 1) * P],
                                                rhs=pT_sb[b_][:, half * 512:(half + 1) * 512],
                                                start=first, stop=last))
                                if half == 0:
                                    b.o("tensor", fnP)
                                else:
                                    n_pv = b.oi("tensor", fnP, "t")
                            pT_pv[b_] = n_pv
                            n_pv_last = n_pv
                            b.w("vector", "s", n_exp)
                            n_tmp = b.oi("vector", lambda e, b_=b_: e.tensor_tensor(
                                out=tmp_bf[:], in0=pT_sb[b_][:, 0:512], in1=pT_sb[b_][:, 512:1024],
                                op=ALU.add), "v")
                            pT_T[b_] = n_tmp
                            if p == 0:
                                b.w("vector", "t", den_of[qb % 2])
                                n_T = b.oi("vector", lambda e, qb=qb:
                                           e.tensor_copy(out=Tacc[qb % 2][:], in_=tmp_bf[:]), "v")
                            else:
                                n_T = b.oi("vector", lambda e, qb=qb:
                                           e.tensor_tensor(out=Tacc[qb % 2][:], in0=Tacc[qb % 2][:],
                                                           in1=tmp_bf[:], op=ALU.add), "v")

                        prev = None
                        for p in range(NP):
                            if p == 4 and pending_tail[0] is not None:
                                pending_tail[0]()
                                pending_tail[0] = None
                            n_exp = emit_S_exp(p)
                            if prev is not None:
                                emit_PV_T(*prev)
                            prev = (p, n_exp)
                        emit_PV_T(*prev)

                        def make_tail(g=g, qb=qb, n_T=n_T, n_pv_last=n_pv_last):
                            def tail():
                                b.w("tensor", "v", n_T)
                                b.w("tensor", "s", n_bclast_cell[0])
                                n_den = b.oi("tensor", lambda e, qb=qb:
                                             e.matmul(out=psum_bc[0:1, :], lhsT=ones_col_f[:],
                                                      rhs=Tacc[qb % 2][:], start=True, stop=True), "t")
                                den_of[qb % 2] = n_den
                                b.w("vector", "t", max(n_den, n_bc_cell[0]))
                                n_rec = b.oi("vector", lambda e: e.reciprocal(out=invden[:],
                                                                              in_=psum_bc[0:1, :]), "v")
                                n_recip_cell[0] = n_rec
                                b.w("tensor", "v", n_rec)
                                n_bc_ = b.oi("tensor", lambda e: e.matmul(out=psum_bc[:],
                                                                          lhsT=ones_row_f[:], rhs=invden[:],
                                                                          start=True, stop=True), "t")
                                n_bc_cell[0] = n_bc_
                                b.w("scalar", "t", n_bc_)
                                n_bcv = b.oi("scalar", lambda e, qb=qb:
                                             e.copy(out=bcast_sb[qb % 2][:], in_=psum_bc[:]), "s")
                                n_bclast_cell[0] = n_bcv
                                b.w("vector", "s", n_bcv)
                                b.w("vector", "t", n_pv_last)
                                n_cn = b.oi("vector", lambda e, g=g, qb=qb: e.tensor_tensor(
                                    out=ctxT_sb[:, g * SQ + qb * 512: g * SQ + (qb + 1) * 512],
                                    in0=psum_ctx[qb % 2][:], in1=bcast_sb[qb % 2][:], op=ALU.mult), "v")
                                ctxnorm_of[qb % 2] = n_cn
                                n_ctxnorm_cell[0] = n_cn
                            return tail
                        pending_tail[0] = make_tail()
                    if pending_tail[0] is not None:
                        pending_tail[0]()
                        pending_tail[0] = None
                    n_bclast = n_bclast_cell[0]
                    n_recip = n_recip_cell[0]
                    n_bc = n_bc_cell[0]
                    n_ctxnorm = n_ctxnorm_cell[0]
                    if stage not in ("qkv", "a"):
                        b.w("gpsimd", "v", n_ctxnorm)
                        b.w("gpsimd", "c1" if g == 0 else "c2", 1)
                        n_cc2in = b.dma("gpsimd", cc2_in[g][:], ctxT_sb[:, g * SQ:(g + 1) * SQ])
                        b.w("gpsimd", "sg", n_cc2in)
                        b.oc("gpsimd", lambda e, g=g: e.collective_compute(
                            "AllGather", ALU.bypass, replica_groups=RG,
                            ins=[cc2_in[g][:]], outs=[cc2_out[g][:]]),
                            "c2" if g == 0 else "c2b")

                n_B_done = n_qkvmm.get((GL * NQB - 1, 2), 0)
                n_C_end = {"t": b.ctr["t"], "s": b.ctr["s"], "v": b.ctr["v"], "sg": b.ctr["sg"]}

        if stage in ("qkv", "a"):
            pass
        elif stage == "ctx":
            tmp = sb("dbgctx", [P, GL * SQ], BF16)
            b.w("sync", "c2", 1)
            b.w("sync", "c2b", 1)
            for g in range(GL):
                b.dma("sync", tmp[:, g * SQ:(g + 1) * SQ], cc2_in[g][:])
            b.w("gpsimd", "sx", b.ctr["sx"])
            for g in range(GL):
                b.dma("gpsimd", dbg["d_ctx"][g * P:(g + 1) * P, :], tmp[:, g * SQ:(g + 1) * SQ])
            b.w("gpsimd", "sg", b.ctr["sg"])
        else:
            with contextlib.ExitStack() as stkD:
                sbD = lambda name, shape, dt: stkD.enter_context(nc.sbuf_tensor(name, shape, dt))
                psD = lambda name, shape, dt: stkD.enter_context(nc.psum_tensor(name, shape, dt))
                x_sb = sbD("x_sb", [P, NT * H], F32)
                psum_d = [psD(f"psd{i}", [P, 512], F32) for i in range(2)]
                psum_bc_d = psD("psbcd", [P, 512], F32)
                psum_trD = psD("pstrD", [P, 512], BF16)

                psd_free = {0: None, 1: None}
                d1scope = contextlib.ExitStack()
                with d1scope:
                    sb1 = lambda name, shape, dt: d1scope.enter_context(nc.sbuf_tensor(name, shape, dt))
                    ctx_sb = sb1("ctx_sb", [P, HC * TOK], BF16)
                    wproj_sb = [sb1(f"wproj{i}", [P, HC * 512], BF16) for i in range(2)]

                    b.w("sync", "t", n_C_end["t"])
                    n_xl = b.dma("sync", x_sb[:].rearrange("p (c d) -> p c d", d=H),
                                 xs[:].rearrange("(c p) d -> p c d", p=P))
                    b.w("sync", "c2", 1)
                    n_ctxl = 0
                    for cc in [c for c in range(HC) if c % 2 == 0]:
                        n_ctxl = b.oi("sync", lambda e, cc=cc: e.dma_start(
                            out=ctx_sb[:, cc * TOK:(cc + 1) * TOK],
                            in_=cc2_out[0][(cc // 2) * P:(cc // 2 + 1) * P,
                                           bass.ts(pid_cell["v"], TOK)]), "sx", 16)
                    b.w("sync", "c2b", 1)
                    for cc in [c for c in range(HC) if c % 2 == 1]:
                        n_ctxl = b.oi("sync", lambda e, cc=cc: e.dma_start(
                            out=ctx_sb[:, cc * TOK:(cc + 1) * TOK],
                            in_=cc2_out[1][(cc // 2) * P:(cc // 2 + 1) * P,
                                           bass.ts(pid_cell["v"], TOK)]), "sx", 16)

                    b.w("tensor", "s", n_C_end["s"])
                    b.w("tensor", "v", n_C_end["v"])
                    n_bse = 0
                    for o4 in range(4):
                        b.w("tensor", "s", n_bse)
                        n_bmm = b.oi("tensor", lambda e, o4=o4: e.matmul(
                            out=psum_bc_d[:], lhsT=ones_row_f[:],
                            rhs=bprojrow[0:1, o4 * 512:(o4 + 1) * 512], start=True, stop=True), "t")
                        b.w("scalar", "t", n_bmm)
                        n_bse = b.oi("scalar", lambda e, o4=o4: e.copy(
                            out=b_sb[:, o4 * 512:(o4 + 1) * 512], in_=psum_bc_d[:]), "s")
                    n_xb = [0] * NT
                    for tt in range(NT):
                        b.w("vector", "sx", n_xl)
                        b.w("vector", "s", n_bse)
                        n_xb[tt] = b.oi("vector", lambda e, tt=tt: e.tensor_tensor(
                            out=x_sb[:, tt * H:(tt + 1) * H], in0=x_sb[:, tt * H:(tt + 1) * H],
                            in1=b_sb[:], op=ALU.add), "v")

                    b.w("gpsimd", "sg", n_cc2in)
                    n_d1 = {}
                    for ob in range(4):
                        if ob >= 2:
                            b.w("gpsimd", "t", n_d1.get((ob - 2, NT - 1), 0))
                        n_wl = b.dma("gpsimd", wproj_sb[ob % 2][:].rearrange("p (c o) -> p c o", o=512),
                                     wprojT[:, ob * 512:(ob + 1) * 512].rearrange("(c p) o -> p c o", p=P))
                        b.w("tensor", "sx", n_ctxl)
                        b.w("tensor", "sg", n_wl)
                        for tt in range(NT):
                            pb = (ob * NT + tt) % 2
                            b.wt("tensor", psd_free[pb])
                            for cc in range(HC):
                                fn = (lambda e, cc=cc, tt=tt, pb=pb, wb=wproj_sb[ob % 2],
                                      first=(cc == 0), last=(cc == HC - 1):
                                      e.matmul(out=psum_d[pb][:],
                                               lhsT=ctx_sb[:, cc * TOK + tt * P: cc * TOK + (tt + 1) * P],
                                               rhs=wb[:, cc * 512:(cc + 1) * 512],
                                               start=first, stop=last))
                                if cc < HC - 1:
                                    b.o("tensor", fn)
                                else:
                                    n_mm = b.oi("tensor", fn, "t")
                            n_d1[(ob, tt)] = n_mm
                            b.w("vector", "t", n_mm)
                            b.w("vector", "v", n_xb[tt])
                            n_ev = b.oi("vector", lambda e, tt=tt, ob=ob, pb=pb: e.tensor_tensor(
                                out=x_sb[:, tt * H + ob * 512: tt * H + (ob + 1) * 512],
                                in0=psum_d[pb][:],
                                in1=x_sb[:, tt * H + ob * 512: tt * H + (ob + 1) * 512],
                                op=ALU.add), "v")
                            psd_free[pb] = ("v", n_ev)

                ln2T_sb = sbD("ln2T", [P, HC * TOK], BF16)
                h1T_sb = sbD("h1T", [P, HC * TOK], BF16)
                alphaE_sb = sbD("alphaE", [P, 4 * TOK], BF16)
                avxT_sb = sbD("avxT", [P, 4 * TOK], BF16)
                uflat_sb = sbD("uflat_b", [P, 4 * H], BF16)

                n_ln2 = [0] * NT
                n_tr2last = [0] * NT
                n_tr2evac = 0
                with contextlib.ExitStack() as stk2:
                    sb2 = lambda name, shape, dt: stk2.enter_context(nc.sbuf_tensor(name, shape, dt))
                    ln2_bf = [sb2(f"ln2bf{i}", [P, H], BF16) for i in range(2)]
                    sq_scrD = sb2("sq_scrD", [P, H], F32)
                    ssumD = sb2("ssumD", [P, 1], F32)
                    negmeanD = [sb2(f"negmeanD{i}", [P, 1], F32) for i in range(2)]
                    varsumD = [sb2(f"varsumD{i}", [P, 1], F32) for i in range(2)]
                    stdvD = [sb2(f"stdvD{i}", [P, 1], F32) for i in range(2)]
                    rstdD = [sb2(f"rstdD{i}", [P, 1], F32) for i in range(2)]
                    nmrD = [sb2(f"nmrD{i}", [P, 1], F32) for i in range(2)]
                    for tt in range(NT):
                        if tt >= 2:
                            b.w("vector", "s", n_ln2[tt - 2])
                        n_redD = b.oi("vector", lambda e, tt=tt: e.reduce_sum(
                            out=ssumD[:], in_=x_sb[:, tt * H:(tt + 1) * H], axis=AX.X), "v")
                        b.w("vector", "v", n_redD)
                        n_nm = b.oi("vector", lambda e, tt=tt: e.tensor_single_scalar(
                            out=negmeanD[tt % 2][:], in_=ssumD[:], scalar=-1.0 / H, op=ALU.mult), "v")
                        b.w("scalar", "v", n_nm)
                        if tt >= 2:
                            b.w("scalar", "t", n_tr2last[tt - 2])
                        n_sqaD = b.oi("scalar", lambda e, tt=tt: e.activation(
                            out=sq_scrD[:], in_=x_sb[:, tt * H:(tt + 1) * H], func=AF.Square,
                            bias=negmeanD[tt % 2][:], scale=1.0, accum_out=varsumD[tt % 2][:]), "s")
                        b.w("scalar", "s", n_sqaD)
                        n_std = b.oi("scalar", lambda e, tt=tt: e.activation(
                            out=stdvD[tt % 2][:], in_=varsumD[tt % 2][:], func=AF.Sqrt,
                            bias=eps_sb[:], scale=1.0 / H), "s")
                        b.w("vector", "s", n_std)
                        n_recD = b.oi("vector", lambda e, tt=tt: e.reciprocal(out=rstdD[tt % 2][:],
                                                                              in_=stdvD[tt % 2][:]), "v")
                        b.w("vector", "v", n_recD)
                        n_nmr = b.oi("vector", lambda e, tt=tt: e.tensor_mul(
                            out=nmrD[tt % 2][:], in0=negmeanD[tt % 2][:], in1=rstdD[tt % 2][:]), "v")
                        b.w("scalar", "v", n_nmr)
                        n_ln2[tt] = b.oi("scalar", lambda e, tt=tt: e.activation(
                            out=ln2_bf[tt % 2][:], in_=x_sb[:, tt * H:(tt + 1) * H], func=AF.Identity,
                            bias=nmrD[tt % 2][:], scale=rstdD[tt % 2][:]), "s")
                        b.w("tensor", "s", n_ln2[tt])
                        for grp in range(4):
                            b.w("tensor", "v", n_tr2evac)
                            for i in range(4):
                                hc = grp * 4 + i
                                n_tr2last[tt] = b.oi("tensor", lambda e, tt=tt, hc=hc, i=i:
                                                     e.transpose(out=psum_trD[:, i * P:(i + 1) * P],
                                                                 in_=ln2_bf[tt % 2][:, hc * P:(hc + 1) * P],
                                                                 identity=ident_bf[:]), "t")
                            b.w("vector", "t", n_tr2last[tt])
                            n_tr2evac = b.oi("vector", lambda e, tt=tt, grp=grp: e.tensor_copy(
                                out=ln2T_sb[:].rearrange("p (c t) -> p c t", t=TOK)
                                    [:, grp * 4:(grp + 1) * 4, tt * P:(tt + 1) * P],
                                in_=psum_trD[:].rearrange("p (i t) -> p i t", t=P)), "v")

                n_h1 = {}
                with contextlib.ExitStack() as stk3:
                    sb3 = lambda name, shape, dt: stk3.enter_context(nc.sbuf_tensor(name, shape, dt))
                    w1_sb = [sb3(f"w1sb{i}", [P, H], BF16) for i in range(2)]
                    b.w("gpsimd", "t", n_tr2last[NT - 1])
                    for ec in range(HC):
                        if ec >= 2:
                            b.w("gpsimd", "t", n_h1.get((ec - 2, 1), 0))
                        n_wl = b.dma("gpsimd", w1_sb[ec % 2][:].rearrange("p (c e) -> p c e", e=P),
                                     w1[:, ec * P:(ec + 1) * P].rearrange("(c p) e -> p c e", p=P))
                        b.w("tensor", "sg", n_wl)
                        b.w("tensor", "v", n_tr2evac)
                        for tb in range(2):
                            pb = (ec * 2 + tb) % 2
                            b.wt("tensor", psd_free[pb])
                            for dc in range(HC):
                                fn = (lambda e, dc=dc, tb=tb, pb=pb, ec=ec,
                                      first=(dc == 0), last=(dc == HC - 1):
                                      e.matmul(out=psum_d[pb][:],
                                               lhsT=w1_sb[ec % 2][:, dc * P:(dc + 1) * P],
                                               rhs=ln2T_sb[:, dc * TOK + tb * 512:
                                                           dc * TOK + (tb + 1) * 512],
                                               start=first, stop=last))
                                if dc < HC - 1:
                                    b.o("tensor", fn)
                                else:
                                    n_mm = b.oi("tensor", fn, "t")
                            n_h1[(ec, tb)] = n_mm
                            b.w("scalar", "t", n_mm)
                            n_ev = b.oi("scalar", lambda e, ec=ec, tb=tb, pb=pb: e.activation(
                                out=h1T_sb[:, ec * TOK + tb * 512: ec * TOK + (tb + 1) * 512],
                                in_=psum_d[pb][:], func=AF.Relu, bias=b1c_sb[:, ec:ec + 1],
                                scale=1.0), "s")
                            psd_free[pb] = ("s", n_ev)

                n_atr = 0
                n_aevac = n_tr2evac
                with contextlib.ExitStack() as stk4:
                    sb4 = lambda name, shape, dt: stk4.enter_context(nc.sbuf_tensor(name, shape, dt))
                    l_sb = sb4("l_sb", [P, 8], F32)
                    negmaxA = sb4("negmaxA", [P, 1], F32)
                    e8 = sb4("e8", [P, 8], F32)
                    denA = sb4("denA", [P, 1], F32)
                    invA = sb4("invA", [P, 1], F32)
                    alpha_bf = sb4("alpha_bf", [P, 8], BF16)
                    b.w("tensor", "sg", n_w2l)
                    b.w("tensor", "s", n_bse)
                    n_b2mm = b.oi("tensor", lambda e: e.matmul(out=psum_bc_d[:, 0:8], lhsT=ones_row_f[:],
                                                               rhs=b2row[:], start=True, stop=True), "t")
                    b.w("scalar", "t", n_b2mm)
                    n_b2e = b.oi("scalar", lambda e: e.copy(out=b2b_sb[:], in_=psum_bc_d[:, 0:8]), "s")
                    n_e8 = 0
                    n_ab = 0
                    n_recA = 0
                    n_atr_d4 = n_tr2last[NT - 1]
                    for tt in range(NT):
                        pb = tt % 2
                        b.wt("tensor", psd_free[pb])
                        for ec in range(HC):
                            fn = (lambda e, ec=ec, tt=tt, pb=pb,
                                  first=(ec == 0), last=(ec == HC - 1):
                                  e.matmul(out=psum_d[pb][:, 0:8],
                                           lhsT=h1T_sb[:, ec * TOK + tt * P: ec * TOK + (tt + 1) * P],
                                           rhs=w2_sb[:, ec * 8:(ec + 1) * 8],
                                           start=first, stop=last))
                            if ec < HC - 1:
                                b.o("tensor", fn)
                            else:
                                n_mm = b.oi("tensor", fn, "t")
                        b.w("vector", "t", n_mm)
                        b.w("vector", "s", max(n_b2e, n_e8))
                        n_l = b.oi("vector", lambda e, pb=pb: e.tensor_tensor(
                            out=l_sb[:], in0=psum_d[pb][:, 0:8], in1=b2b_sb[:], op=ALU.add), "v")
                        psd_free[pb] = ("v", n_l)
                        n_rm = b.oi("vector", lambda e: e.reduce_max(out=negmaxA[:], in_=l_sb[:],
                                                                     axis=AX.X, negate=True), "v")
                        b.w("scalar", "v", max(n_rm, n_recA, n_ab))
                        n_e8 = b.oi("scalar", lambda e: e.activation(out=e8[:], in_=l_sb[:], func=AF.Exp,
                                                                     bias=negmaxA[:], scale=1.0,
                                                                     accum_out=denA[:]), "s")
                        b.w("vector", "s", n_e8)
                        n_recA = b.oi("vector", lambda e: e.reciprocal(out=invA[:], in_=denA[:]), "v")
                        b.w("vector", "v", n_recA)
                        b.w("vector", "t", n_atr)
                        n_ab = b.oi("vector", lambda e: e.tensor_scalar_mul(out=alpha_bf[:], in0=e8[:],
                                                                            scalar1=invA[:]), "v")
                        b.w("tensor", "v", max(n_ab, n_aevac))
                        b.w("tensor", "t", n_atr_d4)
                        n_atr = b.oi("tensor", lambda e: e.transpose(out=psum_trD[0:8, 0:P],
                                                                     in_=alpha_bf[:],
                                                                     identity=ident_bf[:]), "t")
                        n_atr_d4 = n_atr
                        b.w("vector", "t", n_atr)
                        n_aevac = b.oi("vector", lambda e, tt=tt: e.tensor_copy(
                            out=alphaT_sb[:, tt * P:(tt + 1) * P], in_=psum_trD[0:8, 0:P]), "v")

                n_aE = n_b2e
                for krc in range(4):
                    for tb in range(2):
                        b.w("tensor", "v", n_aevac)
                        b.w("tensor", "s", n_aE)
                        n_mm = b.oi("tensor", lambda e, krc=krc, tb=tb: e.matmul(
                            out=psum_bc_d[:], lhsT=sel_sb[:, krc * P:(krc + 1) * P],
                            rhs=alphaT_sb[:, tb * 512:(tb + 1) * 512], start=True, stop=True), "t")
                        b.w("scalar", "t", n_mm)
                        n_aE = b.oi("scalar", lambda e, krc=krc, tb=tb: e.copy(
                            out=alphaE_sb[:, krc * TOK + tb * 512: krc * TOK + (tb + 1) * 512],
                            in_=psum_bc_d[:]), "s")

                n_avx = 0
                with contextlib.ExitStack() as stk6:
                    sb6 = lambda name, shape, dt: stk6.enter_context(nc.sbuf_tensor(name, shape, dt))
                    vflat_sb = sb6("vflat", [P, HC * 512], BF16)
                    b.w("gpsimd", "t", n_atr)
                    b.w("gpsimd", "s", n_aE)
                    n_vfl = b.dma("gpsimd", vflat_sb[:].rearrange("p (c k) -> p c k", k=512),
                                  vflatT[:].rearrange("(c p) k -> p c k", p=P))
                    n_ufl = b.dma("gpsimd", uflat_sb[:].rearrange("p (c d) -> p c d", d=H),
                                  uflat[:].rearrange("(c p) d -> p c d", p=P))

                    for krc in range(4):
                        for tb in range(2):
                            pb = (krc * 2 + tb) % 2
                            b.w("tensor", "sg", n_vfl)
                            b.wt("tensor", psd_free[pb])
                            for dc in range(HC):
                                fn = (lambda e, dc=dc, krc=krc, tb=tb, pb=pb,
                                      first=(dc == 0), last=(dc == HC - 1):
                                      e.matmul(out=psum_d[pb][:],
                                               lhsT=vflat_sb[:, dc * 512 + krc * P:
                                                             dc * 512 + (krc + 1) * P],
                                               rhs=ln2T_sb[:, dc * TOK + tb * 512:
                                                           dc * TOK + (tb + 1) * 512],
                                               start=first, stop=last))
                                if dc < HC - 1:
                                    b.o("tensor", fn)
                                else:
                                    n_mm = b.oi("tensor", fn, "t")
                            b.w("vector", "t", n_mm)
                            b.w("vector", "s", n_aE)
                            n_avx = b.oi("vector", lambda e, krc=krc, tb=tb, pb=pb: e.tensor_tensor(
                                out=avxT_sb[:, krc * TOK + tb * 512: krc * TOK + (tb + 1) * 512],
                                in0=psum_d[pb][:],
                                in1=alphaE_sb[:, krc * TOK + tb * 512: krc * TOK + (tb + 1) * 512],
                                op=ALU.mult), "v")
                            psd_free[pb] = ("v", n_avx)

                with contextlib.ExitStack() as stk7:
                    sb7 = lambda name, shape, dt: stk7.enter_context(nc.sbuf_tensor(name, shape, dt))
                    out_sb = [sb7(f"outsb{i}", [P, H], F32) for i in range(2)]
                    n_store = [0] * NT
                    for tt in range(NT):
                        n_ev = 0
                        for ob in range(4):
                            pb = (tt * 4 + ob) % 2
                            b.w("tensor", "sg", n_ufl)
                            b.w("tensor", "v", n_avx)
                            b.wt("tensor", psd_free[pb])
                            for krc in range(4):
                                fn = (lambda e, krc=krc, tt=tt, ob=ob, pb=pb,
                                      first=(krc == 0), last=(krc == 3):
                                      e.matmul(out=psum_d[pb][:],
                                               lhsT=avxT_sb[:, krc * TOK + tt * P:
                                                            krc * TOK + (tt + 1) * P],
                                               rhs=uflat_sb[:, krc * H + ob * 512:
                                                            krc * H + (ob + 1) * 512],
                                               start=first, stop=last))
                                if krc < 3:
                                    b.o("tensor", fn)
                                else:
                                    n_mm = b.oi("tensor", fn, "t")
                            b.w("vector", "t", n_mm)
                            if tt >= 2 and ob == 0:
                                b.w("vector", "sg", n_store[tt - 2])
                            n_ev = b.oi("vector", lambda e, tt=tt, ob=ob, pb=pb: e.tensor_tensor(
                                out=out_sb[tt % 2][:, ob * 512:(ob + 1) * 512],
                                in0=psum_d[pb][:],
                                in1=x_sb[:, tt * H + ob * 512: tt * H + (ob + 1) * 512],
                                op=ALU.add), "v")
                            psd_free[pb] = ("v", n_ev)
                        b.w("gpsimd", "v", n_ev)
                        n_store[tt] = b.dma("gpsimd", out[tt * P:(tt + 1) * P, :], out_sb[tt % 2][:])
                    b.w("gpsimd", "sg", n_store[NT - 1])

        with nc.Block() as block:
            @block.sync
            def _(eng):
                for fn in b.prog["sync"]:
                    fn(eng)

            @block.tensor
            def _(eng):
                for fn in b.prog["tensor"]:
                    fn(eng)

            @block.scalar
            def _(eng):
                for fn in b.prog["scalar"]:
                    fn(eng)

            @block.vector
            def _(eng):
                for fn in b.prog["vector"]:
                    fn(eng)

            @block.gpsimd
            def _(eng):
                for fn in b.prog["gpsimd"]:
                    fn(eng)

    return nc


def _host_inputs(inputs):
    f32 = np.float32
    q = np.asarray(inputs["L0_attn_query"], dtype=f32).reshape(SQ, H)
    Wqkv = np.asarray(inputs["L0_attn_Wqkv"], dtype=f32)
    Wproj = np.asarray(inputs["L0_attn_Wproj"], dtype=f32)
    bproj = np.asarray(inputs["L0_attn_bproj"], dtype=f32).reshape(1, H)
    U = np.asarray(inputs["L0_ffn_U"], dtype=f32)
    V = np.asarray(inputs["L0_ffn_V"], dtype=f32)
    W1 = np.asarray(inputs["L0_ffn_W1"], dtype=f32)
    b1 = np.asarray(inputs["L0_ffn_b1"], dtype=f32)
    W2 = np.asarray(inputs["L0_ffn_W2"], dtype=f32)
    b2 = np.asarray(inputs["L0_ffn_b2"], dtype=f32).reshape(1, 8)

    WqkvT = np.ascontiguousarray(Wqkv.T)
    WprojT = np.ascontiguousarray(Wproj.T)
    b1c = np.ascontiguousarray(b1.reshape(HC, P).T)
    VflatT = np.ascontiguousarray(V.reshape(512, H).T)
    Uflat = np.ascontiguousarray(U.transpose(0, 2, 1).reshape(512, H))
    sel = np.zeros((8, 512), dtype=f32)
    for k in range(8):
        sel[k, k * 64:(k + 1) * 64] = 1.0
    identity = np.eye(P, dtype=f32)

    in_maps = []
    for c in range(8):
        in_maps.append({
            "xs": np.ascontiguousarray(q[c * TOK:(c + 1) * TOK, :]),
            "wqkvT": np.ascontiguousarray(WqkvT[:, c * GL * 3 * P:(c + 1) * GL * 3 * P]),
            "wprojT": WprojT,
            "bproj": bproj,
            "w1": W1,
            "b1c": b1c,
            "w2": W2,
            "b2": b2,
            "vflatT": VflatT,
            "uflat": Uflat,
            "sel": sel,
            "ident": identity,
        })
    return in_maps


def kernel(**inputs):
    inputs.pop("L0_attn_key", None)
    inputs.pop("L0_attn_value", None)
    nc = build("full")
    in_maps = _host_inputs(inputs)
    res = run_bass_kernel_spmd(nc, in_maps, core_ids=list(range(8)))
    full = np.concatenate([res.results[c]["out"] for c in range(8)], axis=0)
    return full.reshape(1, SQ, H)
